# version 38
# baseline (speedup 1.0000x reference)
import numpy as np
from contextlib import ExitStack

import concourse.bass as bass
import concourse.mybir as mybir
from concourse.bass_utils import run_bass_kernel_spmd

F32 = mybir.dt.float32
F16 = mybir.dt.float16
U8 = mybir.dt.uint8
AF = mybir.ActivationFunctionType
ALU = mybir.AluOpType
AX = mybir.AxisListType

D = 2048
NCH = 16
NQ = 1024
HALO = 512
NH = NQ + HALO
H = 16
WIN = 640
NT = 8
NMT = 5
SCALE = float(1.0 / np.sqrt(128.0))
_S128 = 1.0 / np.sqrt(np.float64(128.0))
SCALE_HI = float(np.float32(_S128))
SCALE_LO = float(_S128 - np.float64(SCALE_HI))
LORA_S = 1.0 / 16.0
PI = float(np.pi)
TWO_PI = float(2.0 * np.pi)

ENGINES = ("sync", "gpsimd", "tensor", "vector", "scalar")


class Prog:
    def __init__(self):
        self.ops = {e: [] for e in ENGINES}
        self.cnt = {}
        self.tok = {}

    def add(self, eng, fn, wait=(), inc=1, token=None, sem=None):
        sem = sem or eng
        waits = []
        for w in wait:
            assert w in self.tok, f"token {w!r} referenced before defined"
            waits.append(self.tok[w])
        self.cnt[sem] = self.cnt.get(sem, 0) + inc
        if token is not None:
            assert token not in self.tok, f"dup token {token}"
            self.tok[token] = (sem, self.cnt[sem])
        self.ops[eng].append((waits, fn, inc, sem))

    def alias(self, new, old):
        self.tok[new] = self.tok[old]

    def emit(self, block, sems):
        for eng in ENGINES:
            ops = self.ops[eng]
            if not ops:
                continue
            serial = eng in ("vector", "scalar")
            def body(h, eng=eng, ops=ops, serial=serial):
                hi = {}
                own = {s for (_, _, i, s) in ops if i > 0}
                cnt = {}
                for waits, fn, inc, sem in ops:
                    for (wsem, val) in waits:
                        if wsem in own:
                            continue
                        if val > hi.get(wsem, 0):
                            h.wait_ge(sems[wsem], val)
                            hi[wsem] = val
                    if (serial or sem == "gp_c") and inc > 0 and cnt.get(sem, 0) > 0:
                        h.wait_ge(sems[sem], cnt[sem])
                    inst = fn(h)
                    if inst is not None and inc > 0:
                        inst.then_inc(sems[sem], inc)
                        cnt[sem] = cnt.get(sem, 0) + inc
            getattr(block, eng)(body)


def build_nc():
    nc = bass.Bass("TRN2", target_bir_lowering=False, debug=False)

    xT = nc.dram_tensor("xT", [D, NH], F32, kind="ExternalInput")
    w_q = nc.dram_tensor("w_q", [16, 16, 128, 128], F32, kind="ExternalInput")
    w_k = nc.dram_tensor("w_k", [16, 16, 128, 128], F32, kind="ExternalInput")
    w_v = nc.dram_tensor("w_v", [16, 16, 128, 128], F32, kind="ExternalInput")
    w_o = nc.dram_tensor("w_o", [16, 16, 128, 128], F32, kind="ExternalInput")
    w_f = nc.dram_tensor("w_f", [16, 16, 128, 128], F32, kind="ExternalInput")
    a_qkv = nc.dram_tensor("a_qkv", [128, 1280], F32, kind="ExternalInput")
    a_o = nc.dram_tensor("a_o", [16, 128, 16], F32, kind="ExternalInput")
    b_qkv = nc.dram_tensor("b_qkv", [80, 2048], F32, kind="ExternalInput")
    b_o = nc.dram_tensor("b_o", [16, 2048], F32, kind="ExternalInput")
    a_f = nc.dram_tensor("a_f", [16, 128, 128], F32, kind="ExternalInput")
    b_f = nc.dram_tensor("b_f", [128, 2048], F32, kind="ExternalInput")
    g_w = nc.dram_tensor("g_w", [16, 128, 8], F32, kind="ExternalInput")
    g_b = nc.dram_tensor("g_b", [1, 8], F32, kind="ExternalInput")
    f_b = nc.dram_tensor("f_b", [1, 2048], F32, kind="ExternalInput")
    nw_d = nc.dram_tensor("nw", [128, 16], F32, kind="ExternalInput")
    cos_d = nc.dram_tensor("cosin", [128, NH], F32, kind="ExternalInput")
    sin_d = nc.dram_tensor("sinin", [128, NH], F32, kind="ExternalInput")
    mask_d = nc.dram_tensor("mask01", [128, NMT * WIN], F16, kind="ExternalInput")
    id_d = nc.dram_tensor("ident", [128, 128], F32, kind="ExternalInput")
    et_d = nc.dram_tensor("et", [8, 128], F16, kind="ExternalInput")
    sw_d = nc.dram_tensor("sw", [128, 128], F16, kind="ExternalInput")
    o_d = nc.dram_tensor("o", [D, NQ], F32, kind="ExternalOutput")

    P = Prog()
    st = ExitStack()

    def sb(name, shape, dt):
        return st.enter_context(nc.sbuf_tensor(name, shape, dt))

    ps0 = st.enter_context(nc.psum_tensor("ps0", [128, 512], F32))
    psS0 = st.enter_context(nc.psum_tensor("psS0", [128, 1024], F32))
    psS1 = st.enter_context(nc.psum_tensor("psS1", [128, 1024], F32))
    psT0 = st.enter_context(nc.psum_tensor("psT0", [128, 1024], F16))
    psP = st.enter_context(nc.psum_tensor("psP", [128, 512], F32))
    psO = st.enter_context(nc.psum_tensor("psO", [128, 512], F32))

    ASZ = 137376
    arena = st.enter_context(nc.sbuf_tensor("arena", [128, ASZ], U8))
    abase = nc.lookup_mloc(arena).addr

    def carve(name, shape, dtype, off):
        nbytes = int(np.prod(shape[1:])) * mybir.dt.size(dtype)
        assert off % 32 == 0 and off + nbytes <= ASZ, (name, off, nbytes)
        hnd = nc.virtual_sbuf_tensor(name, shape, dtype)
        nc._tensor(hnd.name, hnd.shape, hnd.dtype, type="SB")
        mloc = nc.lookup_mloc(hnd)
        mloc.allocated = True
        mloc.addr = abase + off
        return hnd

    h_sb = carve("h", [128, NCH * NH], F16, 0)
    cosT = carve("cosT", [128, NH], F16, 49152)
    sinT = carve("sinT", [128, NH], F16, 52224)
    maskb = carve("maskb", [128, NMT * WIN], F16, 55296)
    k16 = carve("k16", [128, NH], F16, 61696)
    v16 = carve("v16", [128, NH], F16, 64768)
    q16 = carve("q16", [128, NQ], F16, 67840)
    rt1 = carve("rt1", [128, NH], F16, 69888)
    rt2 = carve("rt2", [128, NH], F16, 72960)
    qro = carve("qro", [128, NQ], F16, 76032)
    kro = carve("kro", [128, NH], F16, 78080)
    vtok = carve("vtok", [128, NH], F16, 81152)
    p_sb = [carve(f"p_sb{i}", [128, WIN], F16, 84224 + i * 1280) for i in range(2)]
    pm_sb = [carve(f"pm_sb{i}", [128, WIN], F16, 86784 + i * 1280) for i in range(2)] + \
            [carve(f"pm_sb{i}", [128, WIN], F16, 120768 + (i - 2) * 1280) for i in range(2, 4)]
    pn_sb = [carve(f"pn_sb{i}", [128, WIN], F16, 89344 + i * 1280) for i in range(2)] + \
            [carve(f"pn_sb{i}", [128, WIN], F16, 123328 + (i - 2) * 1280) for i in range(2, 4)]
    pt_sb = [carve(f"pt_sb{i}", [128, WIN], F16, 91904 + i * 1280) for i in range(2)]
    rmin = [carve(f"rmin{i}", [128, 1], F32, 94464 + i * 32) for i in range(2)]
    rsum = [carve(f"rsum{i}", [128, 1], F32, 94464 + 64 + i * 32) for i in range(2)]
    rrec = [carve(f"rrec{i}", [128, 1], F32, 128960 + i * 32) for i in range(4)]
    bqkv = carve("bqkv", [80, 2048], F16, 94656)
    xa_all = carve("xa_all", [80, NH], F16, 106944)
    srow = carve("srow", [1, NH], F32, 84224)
    rrow = carve("rrow", [1, NH], F32, 90368)
    sres = carve("sres", [128, NCH * NQ], F16, 0)
    yf16 = carve("yf16", [128, NCH * NQ], F16, 32768)
    tmprow = carve("tmprow", [1, 1024], F32, 65536)
    srow2 = carve("srow2", [1, 1024], F32, 69632)
    mrow = carve("mrow", [1, 1024], F32, 73728)
    vrow = carve("vrow", [1, 1024], F32, 77824)
    tchunk = [carve(f"tchunk{i}", [128, NQ], F16, 96512 + i * 2048) for i in range(2)]
    s2f16 = [carve(f"s2f16{i}", [128, NQ], F16, 104704 + i * 2048) for i in range(2)]
    afb = carve("afb", [128, 2048], F16, 113088)
    aqkv_sb = carve("aqkv", [128, 1280], F16, 110016)
    aob = carve("aob", [128, 256], F16, 114624)
    swp = carve("swp", [128, NH], F16, 115136)
    sneg = [carve(f"sneg{i}", [128, WIN], F16, 118208 + i * 1280) for i in range(2)]
    swk = carve("swk", [128, NH], F16, 125888)
    bfb2 = carve("bfb2", [128, 2048], F16, 117184)
    logit = carve("logit", [8, 1024], F32, 121280)
    fbb = carve("fbb", [1, 2048], F16, 125376)
    gwb = carve("gwb", [128, 128], F16, 129472)
    logT = carve("logT", [128, 64], F32, 129728)
    gm1 = carve("gm1", [128, 8], F32, 129984)
    gtmp = carve("gtmp", [128, 64], F32, 130016)
    gind = carve("gind", [128, 64], F32, 130272)
    ge_ = carve("ge", [128, 64], F32, 130528)
    gew = carve("gew", [128, 64], F32, 130784)
    gden = carve("gden", [128, 8], F32, 131040)
    grec = carve("grec", [128, 8], F32, 131072)
    gw16 = carve("gw16", [128, 64], F16, 131104)
    wT_sb = carve("wT", [8, 1024], F16, 131232)
    xaf = carve("xaf", [128, 1024], F16, 133280)
    xaw = carve("xaw", [128, 1024], F16, 135328)

    id8 = sb("id8", [8, 8], F32)
    id16 = sb("id16", [128, 128], F16)
    etb = sb("etb", [8, 128], F16)
    nwb = sb("nwb", [128, 16], F32)
    onecol = sb("onecol", [128, 1], F16)
    onerow16 = sb("onerow16", [1, 512], F16)
    onerow32 = sb("onerow32", [1, 128], F32)
    swb = sb("swb", [128, 128], F16)
    b1e6 = sb("b1e6", [128, 1], F32)
    b1e5 = sb("b1e5", [128, 1], F32)
    gbb = sb("gbb", [1, 8], F16)
    NWC = 5
    wcol = [sb(f"wcol{i}", [128, 2048], F16) for i in range(NWC)]
    xbuf = [sb(f"xbuf{i}", [128, NH], F32) for i in range(3)]
    aoutT = sb("aoutT", [128, NCH * NQ], F16)
    bob = carve("bob", [16, 2048], F16, 98752)
    xa_q = carve("xa_q", [16, NH], F16, 110016)
    ystage = [xbuf[0], xbuf[1]]

    P.add("gpsimd", lambda h: h.dma_start(nwb[:], nw_d[:, :]), inc=16, sem="const")
    P.add("gpsimd", lambda h: h.dma_start(id16[:], id_d[:, :]), inc=16, sem="const")
    P.add("gpsimd", lambda h: h.dma_start(swb[:], sw_d[:, :]), inc=16, sem="const")
    P.add("gpsimd", lambda h: h.dma_start(aqkv_sb[:], a_qkv[:, :]),
          inc=16, token="ld_aqkv", sem="const")
    P.add("gpsimd", lambda h: h.dma_start(
        aob[:].rearrange("p (c r) -> p c r", r=16), a_o[:, :, :].rearrange("c p r -> p c r")),
        inc=16, token="ld_ao", sem="const")
    P.add("gpsimd", lambda h: h.dma_start(gbb[:], g_b[:, :]), inc=16, sem="const")

    P.add("gpsimd", lambda h: h.dma_start(
        gwb[:].rearrange("p (c g) -> p c g", g=8), g_w[:, :, :].rearrange("c p g -> p c g")),
        inc=16, sem="const")

    P.add("vector", lambda h: h.memset(onecol[:], 1.0), token="ones_c")
    P.add("vector", lambda h: h.memset(b1e6[:], 1e-6), token="c_1e6")
    P.add("vector", lambda h: h.memset(b1e5[:], 1e-5), token="c_1e5")
    P.add("vector", lambda h: h.memset(onerow16[:], 1.0), token="ones_r16")
    P.add("vector", lambda h: h.memset(onerow32[:], 1.0), token="ones_r32")

    P.add("gpsimd", lambda h: h.dma_start(cosT[:], cos_d[:, :]), inc=16, sem="const")
    P.add("gpsimd", lambda h: h.dma_start(sinT[:], sin_d[:, :]), inc=16, token="consts", sem="const")

    sqbuf = [rt2, kro]
    xqs = ("sync", "gpsimd", "sync")
    for ci in range(3):
        P.add(xqs[ci % 3], lambda h, ci=ci: h.dma_start(
            xbuf[ci % 3][:], xT[ci * 128:(ci + 1) * 128, :]),
            inc=16, token=f"ldx1_{ci}", sem=f"xb{ci % 3}")
    for ci in range(NCH):
        if ci >= 3:
            P.add(xqs[ci % 3], lambda h, ci=ci: h.dma_start(
                xbuf[ci % 3][:], xT[ci * 128:(ci + 1) * 128, :]),
                wait=(f"xc_{ci - 3}",), inc=16, token=f"ldx1_{ci}", sem=f"xb{ci % 3}")
        P.add("scalar", lambda h, ci=ci: h.copy(
            h_sb[:, ci * NH:(ci + 1) * NH], xbuf[ci % 3][:]),
            wait=(f"ldx1_{ci}",), token=f"xc_{ci}")
        wt = (f"xc_{ci}",) + ((f"x1_done{ci - 2}",) if ci >= 2 else ())
        P.add("vector", lambda h, ci=ci: h.tensor_tensor(
            sqbuf[ci % 2][:], h_sb[:, ci * NH:(ci + 1) * NH],
            h_sb[:, ci * NH:(ci + 1) * NH], op=ALU.mult),
            wait=wt, token=f"xsq_{ci}")
        def mmsq(h, ci=ci):
            last = None
            for p in range(3):
                dst = [ps0[0:1, :], psS0[0:1, 0:512], psS0[0:1, 512:1024]][p]
                last = h.matmul(dst, onecol[:], sqbuf[ci % 2][:, p * 512:(p + 1) * 512],
                                start=(ci == 0), stop=(ci == NCH - 1))
            return last
        P.add("tensor", mmsq, wait=(f"xsq_{ci}", "ones_c"), token=f"x1_done{ci}")
    P.add("sync", lambda h: h.dma_start(maskb[:], mask_d[:, :]), inc=16, sem="const_s")
    P.add("sync", lambda h: h.dma_start(id8[:], id_d[0:8, 0:8]), inc=16, sem="const_s")
    P.add("sync", lambda h: h.dma_start(etb[:], et_d[:, :]), inc=16, sem="const_s", token="consts_s")
    P.add("scalar", lambda h: h.copy(srow[0:1, 0:512], ps0[0:1, :]),
          wait=(f"x1_done{NCH - 1}",), token="ms_e0")
    P.add("scalar", lambda h: h.copy(srow[0:1, 512:1536], psS0[0:1, 0:1024]), token="ms_e2")
    P.add("scalar", lambda h: h.activation(rrow[:], srow[:], AF.Ln, bias=b1e6[0:1, :], scale=1.0 / 2048.0),
          wait=("ms_e2", "c_1e6"), token="rms_ln")
    P.add("scalar", lambda h: h.activation(rrow[:], rrow[:], AF.Exp, scale=-0.5),
          token="rms_rstd")
    def bc_rms(h):
        last = None
        for p in range(3):
            dst = [psS1[:, 0:512], psS1[:, 512:1024], psP[:, 0:512]][p]
            last = h.matmul(dst, onerow32[:], rrow[0:1, p * 512:(p + 1) * 512],
                            start=True, stop=True)
        return last
    P.add("tensor", bc_rms, wait=("rms_rstd", "ones_r32"), token="rms_bc")
    for pp in range(3):
        srcb = [psS1[:, 0:512], psS1[:, 512:1024], psP[:, 0:512]][pp]
        P.add("scalar", lambda h, pp=pp, srcb=srcb: h.copy(
            vtok[:, pp * 512:(pp + 1) * 512], srcb),
            wait=(("rms_bc",) if pp == 0 else ()), token=f"rstd16_{pp}")

    for nm, dst, src in [("bqkv", bqkv, b_qkv), ("bo", bob, b_o)]:
        P.add("gpsimd", lambda h, d=dst, s=src: h.dma_start(d[:], s[:, :]),
              wait=("rms_bc",), inc=16, token=f"ld_{nm}", sem="bload")
    P.alias("blds", "ld_bo")

    for cb in range(0, NCH, 2):
        wts = ["rstd16_2", "consts"]
        def happ(h, cb=cb):
            last = None
            for ci in (cb, cb + 1):
                for p in range(3):
                    sl = h_sb[:, ci * NH + p * 512: ci * NH + (p + 1) * 512]
                    last = h.scalar_tensor_tensor(
                        sl, sl, nwb[:, ci:ci + 1], vtok[:, p * 512:(p + 1) * 512],
                        op0=ALU.mult, op1=ALU.mult)
            return last
        P.add("vector", happ, wait=tuple(wts), token=f"h_{cb + 1}")
        P.alias(f"h_{cb}", f"h_{cb + 1}")

    for p in range(3):
        bank = [ps0[0:80, :], psS0[0:80, 0:512], psS0[0:80, 512:1024]][p]
        for ci in range(NCH):
            def mmxa(h, p=p, bank=bank, ci=ci):
                return h.matmul(bank, aqkv_sb[:, ci * 80:(ci + 1) * 80],
                                h_sb[:, ci * NH + p * 512: ci * NH + (p + 1) * 512],
                                start=(ci == 0), stop=(ci == NCH - 1))
            wt = [f"h_{ci}"]
            if ci == 0:
                wt += ["consts_s", "consts"]
            P.add("tensor", mmxa, wait=tuple(wt),
                  token=(f"xa_m_{p}" if ci == NCH - 1 else None))
        P.add("scalar", lambda h, p=p, bank=bank: h.activation(
            xa_all[:, p * 512:(p + 1) * 512], bank, AF.Copy, scale=LORA_S),
            wait=(f"xa_m_{p}",), token=f"xa_e_{p}")

    for col in range(NWC):
        hh0, j0 = divmod(col, 3)
        wsrc = (w_q, w_k, w_v)[j0]
        P.add("gpsimd", lambda h, b=col % NWC, w=wsrc, mo=hh0: h.dma_start(
            wcol[b][:].rearrange("p (c j) -> p c j", j=128),
            w[mo].rearrange("c p j -> p c j")),
            wait=(("ldx1_15",) if col == 0 else ()), inc=16, token=f"wc_{hh0}_{j0}",
            sem=f"wcs{col % NWC}")

    def mk_attn_items(hh):
        def item_sc(ti):
            def go():
                par = ti % 2
                psS = (psS0, psS1)[par]
                mt = min(ti, 4)
                def sc(h, ti=ti, psS=psS):
                    h.matmul(psS[:, 0:512], qro[:, ti * 128:(ti + 1) * 128],
                             kro[:, ti * 128: ti * 128 + 512], start=True, stop=True)
                    return h.matmul(psS[:, 512:640], qro[:, ti * 128:(ti + 1) * 128],
                                    kro[:, ti * 128 + 512: ti * 128 + 640],
                                    start=True, stop=True)
                waits = [f"ropeq_{hh}", f"ropek_{hh}"]
                if ti >= 2:
                    waits.append(f"ts_{hh}_{ti - 2}")
                    waits.append(f"exp_{hh}_{ti - 2}")
                elif hh > 0:
                    waits.append(f"ts_{hh - 1}_{6 + ti}")
                    waits.append(f"exp_{hh - 1}_{6 + ti}")
                elif ti == 0:
                    waits.append("xa_e_2")
                if ti == 0:
                    waits.append(f"vtr_e_{hh}_0")
                P.add("tensor", sc, wait=tuple(waits), token=f"sc_{hh}_{ti}")
                P.add("vector", lambda h, psS=psS, par=par: h.tensor_scalar(
                    sneg[par][:], psS[:, 0:WIN], -1.0, 3.0e38, op0=ALU.mult, op1=ALU.min,
                    accum_out=rmin[par][:]),
                    wait=(f"sc_{hh}_{ti}",), token=f"ts_{hh}_{ti}")
                P.add("scalar", lambda h, par=par: h.activation(
                    p_sb[par][:], sneg[par][:], AF.Exp, bias=rmin[par][:], scale=-1.0),
                    wait=(f"ts_{hh}_{ti}",), token=f"exp_{hh}_{ti}")
                wpm = [f"exp_{hh}_{ti}", "consts_s", "consts"]
                if ti >= 4:
                    wpm.append(f"pn_{hh}_{ti - 4}")
                elif hh > 0:
                    wpm.append(f"pn_{hh - 1}_{4 + ti}")
                P.add("vector", lambda h, mt=mt, par=par, ti=ti: h.scalar_tensor_tensor(
                    pm_sb[ti % 4][:], p_sb[par][:], 1.0, maskb[:, mt * WIN:(mt + 1) * WIN],
                    op0=ALU.mult, op1=ALU.mult, accum_out=rsum[par][:]),
                    wait=tuple(wpm), token=f"pm_{hh}_{ti}")
                P.add("vector", lambda h: h.drain(), inc=0)
                wrc = []
                if ti >= 4:
                    wrc.append(f"pn_{hh}_{ti - 4}")
                elif hh > 0:
                    wrc.append(f"pn_{hh - 1}_{4 + ti}")
                P.add("vector", lambda h, par=par, ti=ti: h.reciprocal(
                    rrec[ti % 4][:], rsum[par][:]),
                      wait=tuple(wrc), token=f"rc_{hh}_{ti}")
                emit_pn(ti)
            return go

        def emit_pn(ti):
            wpn = [f"rc_{hh}_{ti}"]
            if ti >= 4:
                wpn.append(f"ptr_{hh}_{ti - 4}")
            elif hh > 0:
                wpn.append(f"ptr_{hh - 1}_{4 + ti}")
            P.add("vector", lambda h, ti=ti: h.tensor_scalar(
                pn_sb[ti % 4][:], pm_sb[ti % 4][:], rrec[ti % 4][:], None, op0=ALU.mult),
                wait=tuple(wpn), token=f"pn_{hh}_{ti}")

        def item_ptr(ti):
            def go():
                par = ti % 2
                def ptr(h, ti=ti):
                    last = None
                    for s in range(5):
                        last = h.transpose(psT0[:, s * 128:(s + 1) * 128],
                                           pn_sb[ti % 4][:, s * 128:(s + 1) * 128], id16[:])
                    return last
                waits = [f"pn_{hh}_{ti}"]
                if ti >= 1:
                    waits.append(f"ptevac_{hh}_{ti - 1}")
                else:
                    waits.append(f"vtr_e_{hh}_2")
                P.add("tensor", ptr, wait=tuple(waits), token=f"ptr_{hh}_{ti}")
                P.add("scalar", lambda h, par=par: h.copy(pt_sb[par][:],
                                                          psT0[:, 0:WIN]),
                      wait=(f"ptr_{hh}_{ti}",), token=f"ptevac_{hh}_{ti}")
            return go

        def item_av(ti):
            def go():
                par = ti % 2
                def av(h, ti=ti, par=par):
                    last = None
                    for kt in range(5):
                        last = h.matmul(psO[:, (ti % 4) * 128:(ti % 4 + 1) * 128],
                                        vtok[:, (ti + kt) * 128:(ti + kt + 1) * 128],
                                        pt_sb[par][:, kt * 128:(kt + 1) * 128],
                                        start=(kt == 0), stop=(kt == 4))
                    return last
                waits = [f"ptevac_{hh}_{ti}", f"vtr_e_{hh}_2"]
                if ti == 0 and hh > 0:
                    waits.append(f"oevac_{hh - 1}_1")
                elif ti == 4:
                    waits.append(f"oevac_{hh}_0")
                P.add("tensor", av, wait=tuple(waits), token=f"av_{hh}_{ti}")
                if ti % 4 == 3:
                    P.add("scalar", lambda h, ti=ti: h.copy(
                        aoutT[:, hh * NQ + (ti - 3) * 128: hh * NQ + (ti + 1) * 128],
                        psO[:, :]),
                        wait=(f"av_{hh}_{ti}",), token=f"oevac_{hh}_{ti // 4}")
            return go

        return [item_sc(0), item_sc(1), item_sc(2), item_ptr(0), item_sc(3),
                [item_av(0), item_ptr(1)], item_sc(4), [item_av(1), item_ptr(2)],
                item_sc(5), [item_av(2), item_ptr(3)], item_sc(6),
                [item_av(3), item_ptr(4)], item_sc(7), [item_av(4), item_ptr(5)],
                [item_av(5), item_ptr(6)], [item_av(6), item_ptr(7), item_av(7)]]

    prev_items = []
    tail_q = []
    vtr_q = []

    def run_item(it):
        if isinstance(it, list):
            for f in it:
                f()
        else:
            it()

    for hh in range(H):
        if hh == 1:
            P.add("gpsimd", lambda h: h.dma_start(
                wcol[0][:].rearrange("p (c j) -> p c j", j=128),
                w_v[1].rearrange("c p j -> p c j")),
                wait=("proj_0_0_done",), inc=16, token="wc_1_2", sem="wcs0")
        if hh >= 2:
            for j, wsrc in enumerate((w_q, w_k, w_v)):
                col = 3 * hh + j
                bi = col % NWC
                pcol = col - NWC
                P.add("gpsimd", lambda h, b=bi, w=wsrc, mo=hh: h.dma_start(
                    wcol[b][:].rearrange("p (c j) -> p c j", j=128),
                    w[mo].rearrange("c p j -> p c j")),
                    wait=(f"proj_{pcol // 3}_{pcol % 3}_done",), inc=16,
                    token=f"wc_{hh}_{j}", sem=f"wcs{bi}")

        gi_box = [0]

        def bank_guard(gi):
            if gi >= 2:
                return f"pe_evac_{hh}_{gi - 2}"
            if hh > 0:
                return f"pe_evac_{hh - 1}_{6 if gi == 0 else 12}"
            return "xa_e_0" if gi == 0 else "rstd16_2"
        pbank = [ps0, psP]
        if hh > 0:
            swap_banks = [(psS0[:, 0:512], f"ts_{hh - 1}_6"),
                          (psS0[:, 512:1024], f"ts_{hh - 1}_6"),
                          (psS1[:, 0:512], f"ts_{hh - 1}_7"),
                          (psS1[:, 512:1024], f"ts_{hh - 1}_7"),
                          (psP[:, :], f"pe_evac_{hh}_7")]
        else:
            swap_banks = [(psS0[:, 0:512], "xa_e_2"), (psS0[:, 512:1024], "xa_e_2"),
                          (psS1[:, 0:512], "rstd16_2"), (psS1[:, 512:1024], "rstd16_2"),
                          (psP[:, :], f"pe_evac_{hh}_7")]

        def proj_item(name, j, wbi, p, off, r0, dst16):
            def go():
                gi = gi_box[0]
                bank = pbank[gi % 2][:, :]
                def mmp(h, bank=bank, wbi=wbi, p=p, off=off, r0=r0, hh=hh):
                    last = None
                    for ci in range(NCH):
                        last = h.matmul(
                            bank, wcol[wbi][:, ci * 128:(ci + 1) * 128],
                            h_sb[:, ci * NH + off + p * 512: ci * NH + off + (p + 1) * 512],
                            start=(ci == 0), stop=False)
                    return h.matmul(bank, bqkv[r0:r0 + 16, hh * 128:(hh + 1) * 128],
                                    xa_all[r0:r0 + 16, off + p * 512: off + (p + 1) * 512],
                                    start=False, stop=True)
                P.add("tensor", mmp,
                      wait=(f"wc_{hh}_{j}", "xa_e_2", "blds", bank_guard(gi)),
                      token=f"pe_g_{hh}_{gi}")
                ewaits = [f"pe_g_{hh}_{gi}"]
                if hh > 0:
                    if name == "q" and p == 0:
                        ewaits.append(f"ropeq1_{hh - 1}")
                    if name == "k" and p == 0:
                        ewaits.append(f"ropek1_{hh - 1}")
                P.add("scalar", lambda h, dst16=dst16, p=p, bank=bank: h.copy(
                    dst16[:, p * 512:(p + 1) * 512], bank),
                    wait=tuple(ewaits), token=f"pe_evac_{hh}_{gi}")
                gi_box[0] += 1
            return go

        def swap_item(src16, p):
            def go():
                gi = gi_box[0]
                bank, sguard = swap_banks[gi - 8]
                def mms(h, bank=bank, src16=src16, p=p):
                    return h.matmul(bank, swb[:], src16[:, p * 512:(p + 1) * 512],
                                    start=True, stop=True)
                P.add("tensor", mms, wait=("consts", sguard),
                      token=f"pe_g_{hh}_{gi}")
                ewaits = [f"pe_g_{hh}_{gi}"]
                dst = rt2 if src16 is q16 else swk
                if hh > 0 and p == 0:
                    ewaits.append(f"ropeq2_{hh - 1}" if src16 is q16 else
                                  f"ropek2_{hh - 1}")
                P.add("scalar", lambda h, p=p, dst=dst, bank=bank: h.copy(
                    dst[:, p * 512:(p + 1) * 512], bank),
                    wait=tuple(ewaits), token=f"pe_evac_{hh}_{gi}")
                gi_box[0] += 1
            return go

        def vtr_item(b, pst_free, hh=hh):
            vbank = psS0[:, 0:512] if b == 0 else psT0[:, 0:512]
            def go():
                def vtr(h, b=b, vbank=vbank):
                    last = None
                    for s in range(4):
                        kt = b * 4 + s
                        if b == 0:
                            last = h.matmul(vbank[:, s * 128:(s + 1) * 128],
                                            v16[:, kt * 128:(kt + 1) * 128], id16[:],
                                            start=True, stop=True)
                        else:
                            last = h.transpose(vbank[:, s * 128:(s + 1) * 128],
                                               v16[:, kt * 128:(kt + 1) * 128], id16[:])
                    return last
                waits = [f"pe_evac_{hh}_7", "consts_s", "consts"]
                if b == 0:
                    waits.append(f"pe_evac_{hh}_8")
                elif b == 2:
                    waits.append(f"vtr_e_{hh}_1")
                if pst_free is not None:
                    waits.append(pst_free)
                P.add("tensor", vtr, wait=tuple(waits), token=f"vtr_{hh}_{b}")
                P.add("scalar", lambda h, b=b, vbank=vbank: h.copy(
                    vtok[:, b * 512:(b + 1) * 512], vbank),
                    wait=(f"vtr_{hh}_{b}",), token=f"vtr_e_{hh}_{b}")
            return go

        def rope_q():
            wq1 = [f"pe_evac_{hh}_1", "consts_s", "consts"]
            if hh > 0:
                wq1.append(f"sc_{hh - 1}_7")
            P.add("vector", lambda h: h.scalar_tensor_tensor(
                rt1[:, 0:NQ], q16[:], SCALE, cosT[:, HALO:NH],
                op0=ALU.mult, op1=ALU.mult),
                  wait=tuple(wq1), token=f"ropeq1_{hh}")
            P.add("vector", lambda h: h.scalar_tensor_tensor(
                swp[:, 0:NQ], rt2[:, 0:NQ], SCALE, sinT[:, HALO:NH],
                op0=ALU.mult, op1=ALU.mult),
                  wait=(f"pe_evac_{hh}_9",), token=f"ropeq2_{hh}")
            P.add("vector", lambda h: h.tensor_tensor(qro[:], rt1[:, 0:NQ], swp[:, 0:NQ],
                                                      op=ALU.add), token=f"ropeq_{hh}")

        def rope_k():
            wk1 = [f"pe_evac_{hh}_4", f"ropeq_{hh}"]
            if hh == 0:
                wk1.append(f"x1_done{NCH - 1}")
            elif hh > 0:
                wk1.append(f"sc_{hh - 1}_7")
            P.add("vector", lambda h: h.tensor_tensor(rt1[:], k16[:], cosT[:], op=ALU.mult),
                  wait=tuple(wk1), token=f"ropek1_{hh}")
            P.add("vector", lambda h: h.tensor_tensor(swp[:], swk[:], sinT[:], op=ALU.mult),
                  wait=(f"pe_evac_{hh}_12",), token=f"ropek2_{hh}")
            P.add("vector", lambda h: h.tensor_tensor(kro[:], rt1[:], swp[:], op=ALU.add),
                  token=f"ropek_{hh}")

        seq = (
            [proj_item("q", 0, (3 * hh + 0) % NWC, p, HALO, 0, q16) for p in range(2)]
            + [proj_item("k", 1, (3 * hh + 1) % NWC, p, 0, 32, k16) for p in range(3)]
            + [proj_item("v", 2, (3 * hh + 2) % NWC, p, 0, 64, v16) for p in range(3)]
            + [swap_item(q16, p) for p in range(2)]
            + [swap_item(k16, p) for p in range(3)]
        )
        seq[0]()
        if vtr_q:
            vtr_q[0]()
        if tail_q:
            run_item(tail_q[0])
            run_item(tail_q[1])
            run_item(tail_q[2])
        if vtr_q:
            vtr_q[1]()
        pair_at = {1: (0,), 2: (1, 2), 3: (3, 4), 4: (5, 6), 5: (7, 8),
                   6: (9, 10), 7: (11, 12)}
        for i, g in enumerate(seq[1:], start=1):
            g()
            if prev_items:
                for j in pair_at.get(i, ()):
                    run_item(prev_items[j])
            if i == 1 and vtr_q:
                vtr_q[2]()
        rope_q()
        rope_k()
        P.alias(f"proj_{hh}_0_done", f"pe_g_{hh}_1")
        P.alias(f"proj_{hh}_1_done", f"pe_g_{hh}_4")
        P.alias(f"proj_{hh}_2_done", f"pe_g_{hh}_7")
        items = mk_attn_items(hh)
        tail_q = prev_items[13:16] if len(prev_items) > 13 else []
        prev_items = items
        vtr_q = [vtr_item(0, None),
                 vtr_item(1, f"ptevac_{hh - 1}_7" if hh >= 1 else None),
                 vtr_item(2, None)]

    vtr_q[0]()
    run_item(tail_q[0])
    run_item(tail_q[1])
    run_item(tail_q[2])
    vtr_q[1]()
    vtr_q[2]()
    ep_items = list(prev_items)
    ep_pos = [0]

    def ep_pop(n=1):
        for _ in range(n):
            if ep_pos[0] < len(ep_items):
                run_item(ep_items[ep_pos[0]])
                ep_pos[0] += 1

    lnb = {(0, 0): psS1[0:1, 0:512], (0, 1): psS1[0:1, 512:1024],
           (1, 0): psP[0:1, 0:512], (1, 1): psO[0:1, 0:512]}

    def emit_ln_stat(tag, src_t, ci, extra=()):
        wts = [f"{tag}_src_{ci}"]
        if ci >= 2:
            wts.append(f"{tag}_st_{ci - 2}")
        P.add("vector", lambda h, ci=ci: h.tensor_tensor(
            s2f16[ci % 2][:], src_t[:, ci * NQ:(ci + 1) * NQ],
            src_t[:, ci * NQ:(ci + 1) * NQ], op=ALU.mult),
            wait=tuple(wts), token=f"{tag}_sq_{ci}")
        def mst(h, ci=ci):
            last = None
            for p in range(2):
                last = h.matmul(lnb[(0, p)], onecol[:],
                                src_t[:, ci * NQ + p * 512: ci * NQ + (p + 1) * 512],
                                start=(ci == 0), stop=(ci == NCH - 1))
                last = h.matmul(lnb[(1, p)], onecol[:],
                                s2f16[ci % 2][:, p * 512:(p + 1) * 512],
                                start=(ci == 0), stop=(ci == NCH - 1))
            return last
        P.add("tensor", mst, wait=(f"{tag}_sq_{ci}",) + tuple(extra),
              token=f"{tag}_st_{ci}")

    def emit_ln_finish(tag, src_t, dst16, dst32, stage_dma):
        banksF = {(0, 0): psS1[:, 0:512], (0, 1): psS1[:, 512:1024],
                  (1, 0): psP[:, :], (1, 1): psO[:, 0:512]}
        P.add("scalar", lambda h: h.activation(mrow[0:1, 0:1024], psS1[0:1, 0:1024],
                                               AF.Copy, scale=1.0 / 2048.0),
              wait=(f"{tag}_st_{NCH - 1}",), token=f"{tag}_rm")
        P.add("scalar", lambda h: h.activation(vrow[0:1, 0:512], lnb[(1, 0)],
                                               AF.Copy, scale=1.0 / 2048.0),
              token=f"{tag}_r2")
        P.add("scalar", lambda h: h.activation(vrow[0:1, 512:1024], lnb[(1, 1)],
                                               AF.Copy, scale=1.0 / 2048.0),
              token=f"{tag}_r3")
        P.add("vector", lambda h: h.tensor_tensor(tmprow[:], mrow[:], mrow[:], op=ALU.mult),
              wait=(f"{tag}_rm",), token=f"{tag}_m2")
        P.add("vector", lambda h: h.tensor_tensor(vrow[:], vrow[:], tmprow[:],
                                                  op=ALU.subtract),
              wait=(f"{tag}_r3",), token=f"{tag}_var")
        P.add("scalar", lambda h: h.activation(rrow[0:1, 0:1024], vrow[:], AF.Ln, bias=b1e5[0:1, :]),
              wait=(f"{tag}_var", "c_1e5"), token=f"{tag}_vln")
        P.add("scalar", lambda h: h.activation(rrow[0:1, 0:1024], rrow[0:1, 0:1024], AF.Exp,
                                               scale=-0.5), token=f"{tag}_rstd")
        P.add("vector", lambda h: h.tensor_tensor(mrow[:], mrow[:], rrow[0:1, 0:1024],
                                                  op=ALU.mult),
              wait=(f"{tag}_rstd",), token=f"{tag}_mr")
        def bcs(h):
            h.matmul(banksF[(0, 0)], onerow32[:], rrow[0:1, 0:512], start=True, stop=True)
            h.matmul(banksF[(0, 1)], onerow32[:], rrow[0:1, 512:1024], start=True, stop=True)
            h.matmul(banksF[(1, 0)], onerow32[:], mrow[0:1, 0:512], start=True, stop=True)
            return h.matmul(banksF[(1, 1)], onerow32[:], mrow[0:1, 512:1024],
                            start=True, stop=True)
        P.add("tensor", bcs, wait=(f"{tag}_mr",), token=f"{tag}_bc")
        P.add("scalar", lambda h: h.copy(rt1[:, 0:1024], psS1[:, 0:1024]),
              wait=(f"{tag}_bc",), token=f"{tag}_f0")
        P.add("scalar", lambda h: h.copy(rt2[:, 0:512], psP[:, :]), token=f"{tag}_f1")
        P.add("scalar", lambda h: h.copy(rt2[:, 512:1024], psO[:, 0:512]),
              token=f"{tag}_fb")

        def dst_of(cj):
            if dst16 is not None:
                return dst16, cj * NQ
            return dst32[cj % 2], 0

        for ci in range(NCH + 1):
            def app(h, ci=ci):
                last = None
                if ci < NCH:
                    for p in range(2):
                        last = h.scalar_tensor_tensor(
                            tchunk[ci % 2][:, p * 512:(p + 1) * 512],
                            src_t[:, ci * NQ + p * 512: ci * NQ + (p + 1) * 512], 1.0,
                            rt1[:, p * 512:(p + 1) * 512], op0=ALU.mult, op1=ALU.mult)
                if ci >= 1:
                    d, off = dst_of(ci - 1)
                    for p in range(2):
                        last = h.tensor_tensor(
                            d[:, off + p * 512: off + (p + 1) * 512],
                            tchunk[(ci - 1) % 2][:, p * 512:(p + 1) * 512],
                            rt2[:, p * 512:(p + 1) * 512], op=ALU.subtract)
                return last
            wapp = [f"{tag}_fb"] if ci == 0 else []
            if stage_dma and ci >= 3:
                wapp.append(f"out_{ci - 3}")
            P.add("vector", app, wait=tuple(wapp),
                  token=(f"{tag}_y_{ci - 1}" if ci >= 1 else None))
            if stage_dma and ci >= 1:
                cj = ci - 1
                P.add("sync", lambda h, cj=cj: h.dma_start(
                    o_d[cj * 128:(cj + 1) * 128, :], ystage[cj % 2][:, 0:NQ]),
                    wait=(f"{tag}_y_{cj}",), inc=16, token=f"out_{cj}",
                    sem=f"ob{cj % 2}")


    for p in range(2):
        bank = [ps0[0:16, :], psP[0:16, :]][p]
        for ci in range(NCH):
            def mmxo(h, p=p, bank=bank, ci=ci):
                return h.matmul(bank, aob[:, ci * 16:(ci + 1) * 16],
                                aoutT[:, ci * NQ + p * 512: ci * NQ + (p + 1) * 512],
                                start=(ci == 0), stop=(ci == NCH - 1))
            wts = [f"oevac_{ci}_{p}"]
            if ci == 0:
                wts += ["consts_s", "consts",
                        f"pe_evac_{H - 1}_{6 if p == 0 else 12}"]
            P.add("tensor", mmxo, wait=tuple(wts),
                  token=(f"xo_m_{p}" if ci == NCH - 1 else None))
            if p == 0 and ci < NCH - 1:
                ep_pop(1)
            if p == 0 and ci == NCH - 2:
                ep_pop(1)
        P.add("scalar", lambda h, p=p, bank=bank: h.activation(
            xa_q[0:16, p * 512:(p + 1) * 512], bank, AF.Copy, scale=LORA_S),
            wait=(f"xo_m_{p}",), token=f"xo_e_{p}")

    P.add("gpsimd", lambda h: h.dma_start(
        afb[:].rearrange("p (c e) -> p c e", e=128), a_f[:, :, :].rearrange("c p e -> p c e")),
        wait=("xo_m_1",), inc=16, token="ld_af", sem="af")
    P.add("gpsimd", lambda h: h.dma_start(bfb2[:], b_f[:, :]), wait=("xo_m_1",),
          inc=16, sem="af")
    P.add("gpsimd", lambda h: h.dma_start(fbb[:], f_b[:, :]), wait=("xo_m_1",),
          inc=16, token="ld_bf2", sem="af")
    def emit_woc(mo, waits):
        P.add("gpsimd", lambda h, b=mo % NWC, mo=mo: h.dma_start(
            wcol[b][:].rearrange("p (c j) -> p c j", j=128),
            w_o[mo].rearrange("c p j -> p c j")),
            wait=waits, inc=16, token=f"woc_{mo}", sem=f"wcs{mo % NWC}")

    def emit_ldx3(mo, waits):
        P.add("sync", lambda h, mo=mo: h.dma_start(
            xbuf[mo % 2][:, 0:NQ], xT[mo * 128:(mo + 1) * 128, HALO:NH]),
            wait=waits, inc=16, token=f"ldx3_{mo}", sem=("xb0", "xb2")[mo % 2])

    for mo in range(NWC):
        emit_woc(mo, (f"pe_evac_{H - 1}_7",))
    for mo in range(2):
        emit_ldx3(mo, (f"h_{NCH - 1}",))

    for mo in range(NCH):
        for p in range(2):
            bank = [ps0[:, :], psP[:, :]][p]
            for ci in range(NCH):
                def mmo(h, mo=mo, p=p, bank=bank, ci=ci):
                    return h.matmul(bank, wcol[mo % NWC][:, ci * 128:(ci + 1) * 128],
                                    aoutT[:, ci * NQ + p * 512: ci * NQ + (p + 1) * 512],
                                    start=(ci == 0), stop=False)
                wts = [f"oevac_{ci}_{p}"]
                if ci == 0:
                    wts += [f"woc_{mo}", "blds"]
                    gidx = mo * 2 + p
                    if gidx >= 2:
                        wts.append(f"sres_g_{gidx - 2}")
                    else:
                        wts.append(f"xo_e_{p}")
                P.add("tensor", mmo, wait=tuple(wts))
            def mmo_l(h, mo=mo, p=p, bank=bank):
                return h.matmul(bank, bob[:, mo * 128:(mo + 1) * 128],
                                xa_q[0:16, p * 512:(p + 1) * 512], start=False, stop=True)
            P.add("tensor", mmo_l, wait=(f"xo_e_{p}",), token=f"po_g_{mo}_{p}")
            P.add("vector", lambda h, mo=mo, p=p, bank=bank: h.scalar_tensor_tensor(
                sres[:, mo * NQ + p * 512: mo * NQ + (p + 1) * 512],
                xbuf[mo % 2][:, p * 512:(p + 1) * 512], 2.0, bank,
                op0=ALU.mult, op1=ALU.add),
                wait=(f"po_g_{mo}_{p}", f"ldx3_{mo}"), token=f"sres_g_{gidx}")
        P.alias(f"sres_{mo}", f"sres_g_{mo * 2 + 1}")
        P.alias(f"ln1_src_{mo}", f"sres_{mo}")
        if mo + NWC < NCH:
            emit_woc(mo + NWC, (f"po_g_{mo}_1",))
        if mo + 2 < NCH:
            emit_ldx3(mo + 2, (f"sres_{mo}",))

    for ci in range(NCH):
        emit_ln_stat("ln1", sres, ci,
                     extra=((f"ts_{H - 1}_7", "sres_g_31",
                             f"oevac_{H - 1}_1") if ci == 0 else ()))
    emit_ln_finish("ln1", sres, yf16, None, False)

    lgbank = psS0[0:8, 512:1024]
    for ci in range(NCH):
        def mmlg(h, ci=ci):
            return h.matmul(lgbank, gwb[:, ci * 8:(ci + 1) * 8],
                            yf16[:, ci * NQ: ci * NQ + 512],
                            start=(ci == 0), stop=False)
        wts = [f"ln1_y_{ci}"]
        if ci == 0:
            wts += ["consts_s", "consts", "ones_r16", "ld_bf2"]
        P.add("tensor", mmlg, wait=tuple(wts))
        for p in range(2):
            bank = [ps0[:, :], psS0[:, 0:512]][p]
            def mmxa(h, p=p, bank=bank, ci=ci):
                return h.matmul(bank, afb[:, ci * 128:(ci + 1) * 128],
                                yf16[:, ci * NQ + p * 512: ci * NQ + (p + 1) * 512],
                                start=(ci == 0), stop=(ci == NCH - 1))
            P.add("tensor", mmxa, wait=(f"ln1_y_{ci}",),
                  token=(f"xaf_m_{p}" if ci == NCH - 1 else None))
    for p in range(2):
        bank = [ps0[:, :], psS0[:, 0:512]][p]
        P.add("scalar", lambda h, p=p, bank=bank: h.copy(xaf[:, p * 512:(p + 1) * 512], bank),
              wait=(f"xaf_m_{p}",), token=f"xaf_e_{p}")
    P.add("tensor", lambda h: h.matmul(lgbank, gbb[:], onerow16[0:1, 0:512],
                                       start=False, stop=True), token="lg_0")
    P.add("scalar", lambda h: h.copy(logit[:, 0:512], lgbank),
          wait=("lg_0",), token="lg_e_0")
    for ci in range(NCH):
        def mmlg2(h, ci=ci):
            return h.matmul(lgbank, gwb[:, ci * 8:(ci + 1) * 8],
                            yf16[:, ci * NQ + 512: ci * NQ + 1024],
                            start=(ci == 0), stop=False)
        P.add("tensor", mmlg2, wait=(("lg_e_0",) if ci == 0 else ()))
    P.add("tensor", lambda h: h.matmul(lgbank, gbb[:], onerow16[0:1, 0:512],
                                       start=False, stop=True), token="lg_1")
    P.add("scalar", lambda h: h.copy(logit[:, 512:1024], lgbank),
          wait=("lg_1",), token="lg_e_1")
    def ltr(h):
        last = None
        for ti in range(8):
            last = h.transpose(psO[:, ti * 8:(ti + 1) * 8],
                               logit[0:8, ti * 128:(ti + 1) * 128], id8[:])
        return last
    P.add("tensor", ltr, wait=("lg_e_0", "lg_e_1", "consts_s", "consts", "ln1_r3"), token="ltr")
    P.add("scalar", lambda h: h.copy(logT[:], psO[:, 0:64]), wait=("ltr",), token="ltr_e")
    for ti in range(8):
        P.add("vector", lambda h, ti=ti: h.tensor_reduce(
            gm1[:, ti:ti + 1], logT[:, ti * 8:(ti + 1) * 8], axis=AX.X, op=ALU.max,
            negate=True), wait=(("ltr_e",) if ti == 0 else ()), token=f"g_m1_{ti}")
    for ti in range(8):
        P.add("vector", lambda h, ti=ti: h.tensor_scalar(
            gind[:, ti * 8:(ti + 1) * 8], logT[:, ti * 8:(ti + 1) * 8], gm1[:, ti:ti + 1],
            0.0, op0=ALU.add, op1=ALU.is_ge), token=f"g_i1_{ti}")
    for ti in range(8):
        P.add("vector", lambda h, ti=ti: h.scalar_tensor_tensor(
            gtmp[:, ti * 8:(ti + 1) * 8], gind[:, ti * 8:(ti + 1) * 8], -1.0e9,
            logT[:, ti * 8:(ti + 1) * 8], op0=ALU.mult, op1=ALU.add), token=f"g_lm_{ti}")
    for ti in range(8):
        P.add("vector", lambda h, ti=ti: h.tensor_reduce(
            gden[:, ti:ti + 1], gtmp[:, ti * 8:(ti + 1) * 8], axis=AX.X, op=ALU.max,
            negate=True), token=f"g_m2_{ti}")
    for ti in range(8):
        P.add("scalar", lambda h, ti=ti: h.activation(
            ge_[:, ti * 8:(ti + 1) * 8], logT[:, ti * 8:(ti + 1) * 8], AF.Exp,
            bias=gm1[:, ti:ti + 1]),
            wait=(f"g_i1_{ti}",), token=f"g_e_{ti}")
    for ti in range(8):
        P.add("vector", lambda h, ti=ti: h.tensor_scalar(
            gind[:, ti * 8:(ti + 1) * 8], logT[:, ti * 8:(ti + 1) * 8], gden[:, ti:ti + 1],
            0.0, op0=ALU.add, op1=ALU.is_ge), token=f"g_i2_{ti}")
    for ti in range(8):
        P.add("vector", lambda h, ti=ti: h.scalar_tensor_tensor(
            gew[:, ti * 8:(ti + 1) * 8], ge_[:, ti * 8:(ti + 1) * 8], 1.0,
            gind[:, ti * 8:(ti + 1) * 8], op0=ALU.mult, op1=ALU.mult,
            accum_out=gden[:, ti:ti + 1]),
            wait=(f"g_e_{ti}",), token=f"g_ew_{ti}")
    P.add("vector", lambda h: h.drain(), inc=0)
    P.add("vector", lambda h: h.reciprocal(grec[:], gden[:]), token="g_rc")
    P.add("vector", lambda h: h.drain(), inc=0)
    for ti in range(8):
        P.add("vector", lambda h, ti=ti: h.tensor_scalar(
            gw16[:, ti * 8:(ti + 1) * 8], gew[:, ti * 8:(ti + 1) * 8], grec[:, ti:ti + 1],
            None, op0=ALU.mult), token=f"g_w_{ti}")
    def wtr(h):
        last = None
        for ti in range(8):
            last = h.transpose(psT0[0:8, ti * 128:(ti + 1) * 128],
                               gw16[:, ti * 8:(ti + 1) * 8], id16[:])
        return last

    def emit_wfc(mo, waits):
        P.add("gpsimd", lambda h, b=mo % NWC, mo=mo: h.dma_start(
            wcol[b][:].rearrange("p (c j) -> p c j", j=128),
            w_f[mo].rearrange("c p j -> p c j")),
            wait=waits, inc=16, token=f"wfc_{mo}", sem=f"wcs{mo % NWC}")

    for mo in range(NWC):
        last_user = 15 if mo == 0 else 10 + mo
        emit_wfc(mo, (f"po_g_{last_user}_1",))

    fbank4 = [ps0[:, :], psS0[:, 0:512], psP[:, :], psO[:, 0:512]]

    def emit_mmf_w(mo, p):
        gidx = mo * 2 + p
        bank = fbank4[gidx % 4]
        def mmf_w(h, mo=mo, p=p, bank=bank):
            last = None
            for ci in range(NCH):
                last = h.matmul(bank, wcol[mo % NWC][:, ci * 128:(ci + 1) * 128],
                                yf16[:, ci * NQ + p * 512: ci * NQ + (p + 1) * 512],
                                start=(ci == 0), stop=False)
            return h.matmul(bank, fbb[0:1, mo * 128:(mo + 1) * 128],
                            onerow16[0:1, 0:512], start=False, stop=False)
        waits = [f"wfc_{mo}", "consts_s", "consts", "ld_bf2"]
        if gidx >= 4:
            waits.append(f"s2_g_{gidx - 4}")
        elif gidx < 2:
            waits.append(f"xaf_e_{p}")
        elif gidx == 3:
            waits.append("ltr_e")
        P.add("tensor", mmf_w, wait=tuple(waits), inc=1)

    def emit_mmf_l(mo, p):
        gidx = mo * 2 + p
        bank = fbank4[gidx % 4]
        def mmf_l(h, mo=mo, p=p, bank=bank):
            return h.matmul(bank, bfb2[:, mo * 128:(mo + 1) * 128],
                            xaw[:, p * 512:(p + 1) * 512], start=False, stop=True)
        P.add("tensor", mmf_l, wait=(f"xaw_{p}",), token=f"pf_g_{mo}_{p}")
        P.add("vector", lambda h, mo=mo, p=p, bank=bank: h.scalar_tensor_tensor(
            sres[:, mo * NQ + p * 512: mo * NQ + (p + 1) * 512],
            yf16[:, mo * NQ + p * 512: mo * NQ + (p + 1) * 512], 1.0, bank,
            op0=ALU.mult, op1=ALU.add),
            wait=(f"pf_g_{mo}_{p}",), token=f"s2_g_{mo * 2 + p}")

    emit_mmf_w(0, 0)
    emit_mmf_w(0, 1)
    emit_mmf_w(1, 0)
    emit_mmf_w(1, 1)
    P.add("tensor", wtr, wait=("g_w_7", f"ptevac_{H - 1}_7"), token="wtr")
    P.add("scalar", lambda h: h.copy(wT_sb[:], psT0[0:8, :]), wait=("wtr",), token="wtr_e")
    for p in range(2):
        bank2 = [psS1[:, 0:512], psS1[:, 512:1024]][p]
        P.add("tensor", lambda h, bank2=bank2, p=p: h.matmul(
            bank2, etb[:], wT_sb[:, p * 512:(p + 1) * 512], start=True, stop=True),
            wait=("wtr_e", "consts_s", "consts"), token=f"wbc_{p}")
        P.add("vector", lambda h, p=p, bank2=bank2: h.tensor_tensor(
            xaw[:, p * 512:(p + 1) * 512], xaf[:, p * 512:(p + 1) * 512], bank2,
            op=ALU.mult),
            wait=(f"wbc_{p}", f"xaf_e_{p}"), token=f"xaw_{p}")
    for mo in range(2):
        emit_mmf_l(mo, 0)
        emit_mmf_l(mo, 1)
        P.alias(f"s2_{mo}", f"s2_g_{mo * 2 + 1}")
        P.alias(f"ln2_src_{mo}", f"s2_{mo}")
    emit_wfc(NWC, ("pf_g_0_1",))
    emit_wfc(NWC + 1, ("pf_g_1_1",))
    for mo in range(2, NCH):
        for p in range(2):
            emit_mmf_w(mo, p)
            emit_mmf_l(mo, p)
        P.alias(f"s2_{mo}", f"s2_g_{mo * 2 + 1}")
        P.alias(f"ln2_src_{mo}", f"s2_{mo}")
        if mo + NWC < NCH:
            emit_wfc(mo + NWC, (f"pf_g_{mo}_1",))

    for ci in range(NCH):
        emit_ln_stat("ln2", sres, ci,
                     extra=(("xaw_1", "ltr_e", "ln1_r3", "s2_g_31") if ci == 0 else ()))
    emit_ln_finish("ln2", sres, None, ystage, True)

    with ExitStack() as es:
        extra = (["gp_c", "const", "const_s", "bload", "af", "xb0", "xb1", "xb2",
                  "ob0", "ob1"] +
                 [f"wcs{i}" for i in range(NWC)])
        sems = {e: es.enter_context(nc.semaphore(f"sem_{e}")) for e in ENGINES}
        for e in extra:
            sems[e] = es.enter_context(nc.semaphore(f"sem_{e}"))
        ov0 = P.tok[f"out_{NCH - 2}"][1]
        ov1 = P.tok[f"out_{NCH - 1}"][1]
        P.add("gpsimd", lambda h: h.wait_ge(sems["ob0"], ov0), inc=0)
        P.add("gpsimd", lambda h: h.wait_ge(sems["ob1"], ov1), inc=0)
        block = es.enter_context(nc.Block())
        P.emit(block, sems)
    st.close()
    return nc


_NC_CACHE = None


def _get_nc():
    global _NC_CACHE
    if _NC_CACHE is None:
        _NC_CACHE = build_nc()
    return _NC_CACHE


def _prep_weights(I):
    f32 = np.float32

    def tiles(wT):
        return np.ascontiguousarray(wT.reshape(16, 128, 16, 128).transpose(2, 0, 1, 3))

    out = {}
    out["w_q"] = tiles(np.ascontiguousarray(np.asarray(I["L0_attn_wq"], f32).T))
    out["w_k"] = tiles(np.ascontiguousarray(np.asarray(I["L0_attn_wk"], f32).T))
    out["w_v"] = tiles(np.ascontiguousarray(np.asarray(I["L0_attn_wv"], f32).T))
    out["w_o"] = tiles(np.ascontiguousarray(np.asarray(I["L0_attn_wo"], f32).T))
    out["w_f"] = tiles(np.ascontiguousarray(np.asarray(I["L0_ffn_W"], f32).T))
    aq = np.asarray(I["L0_attn_qA"], f32).T.reshape(16, 128, 16)
    ak = np.asarray(I["L0_attn_kA"], f32).T.reshape(16, 128, 16)
    av = np.asarray(I["L0_attn_vA"], f32).T.reshape(16, 128, 16)
    z16 = np.zeros_like(aq)
    out["a_qkv"] = np.ascontiguousarray(
        np.concatenate([aq, z16, ak, z16, av], axis=2).transpose(1, 0, 2).reshape(128, 1280))
    out["a_o"] = np.ascontiguousarray(np.asarray(I["L0_attn_oA"], f32).T.reshape(16, 128, 16))
    zb = np.zeros((16, 2048), f32)
    out["b_qkv"] = np.ascontiguousarray(np.concatenate(
        [np.asarray(I["L0_attn_qB"], f32).T, zb, np.asarray(I["L0_attn_kB"], f32).T,
         zb, np.asarray(I["L0_attn_vB"], f32).T], axis=0))
    out["b_o"] = np.ascontiguousarray(np.asarray(I["L0_attn_oB"], f32).T)
    out["a_f"] = np.ascontiguousarray(
        np.asarray(I["L0_ffn_lora_A"], f32).reshape(128, 2048).T.reshape(16, 128, 128))
    out["b_f"] = np.ascontiguousarray(
        np.asarray(I["L0_ffn_lora_B"], f32).transpose(0, 2, 1).reshape(128, 2048))
    out["g_w"] = np.ascontiguousarray(np.asarray(I["L0_ffn_gate_w"], f32).T.reshape(16, 128, 8))
    out["g_b"] = np.ascontiguousarray(np.asarray(I["L0_ffn_gate_b"], f32).reshape(1, 8))
    out["f_b"] = np.ascontiguousarray(np.asarray(I["L0_ffn_b"], f32).reshape(1, 2048))
    out["nw"] = np.ascontiguousarray(np.asarray(I["L0_attn_norm_w"], f32).reshape(16, 128).T)
    out["ident"] = np.eye(128, dtype=f32)
    out["et"] = np.repeat(np.eye(8, dtype=np.float16), 16, axis=1)
    SW = np.zeros((128, 128), dtype=np.float16)
    SW[np.arange(64), np.arange(64) + 64] = -1.0
    SW[np.arange(64, 128), np.arange(64, 128) - 64] = 1.0
    out["sw"] = np.ascontiguousarray(SW.T)
    return out


_TABLES = None


def _rope_tables_host():
    global _TABLES
    if _TABLES is None:
        half = 64
        try:
            import jax
            import jax.numpy as jnp
            with jax.default_device(jax.devices("cpu")[0]):
                theta = 1.0 / 10000.0 ** (jnp.arange(half, dtype=jnp.float32) / half)
                pos = jnp.arange(-512, 4096, dtype=jnp.float32)
                freqs = pos[:, None] * theta[None, :]
                _TABLES = (np.asarray(jnp.cos(freqs)), np.asarray(jnp.sin(freqs)))
        except Exception:
            theta = 1.0 / np.float32(10000.0) ** (np.arange(half, dtype=np.float32)
                                                  / np.float32(half))
            pos = np.arange(-512, 4096, dtype=np.float32)
            freqs = (pos[:, None] * theta[None, :]).astype(np.float32)
            _TABLES = (np.cos(freqs), np.sin(freqs))
    return _TABLES


def _mask_for_core(chunk):
    m = np.zeros((128, NMT * WIN), dtype=np.float16)
    r = np.arange(128)[:, None]
    c = np.arange(WIN)[None, :]
    for mt in range(NMT):
        ti = mt
        q_local = 512 + 128 * ti + r
        k_local = 128 * ti + c
        valid = (k_local <= q_local) & (k_local >= q_local - 511)
        if chunk == 0 and mt < 4:
            valid = valid & (k_local >= 512)
        m[:, mt * WIN:(mt + 1) * WIN] = valid.astype(np.float16)
    return m


def kernel(**inputs):
    I = inputs
    x = np.asarray(I["L0_attn_x"], dtype=np.float32)
    B, L, _ = x.shape
    shared = _prep_weights(I)
    nc = _get_nc()
    in_maps = []
    for c in range(8):
        b, chunk = divmod(c, 4)
        s0 = chunk * NQ
        lo = s0 - HALO
        if lo < 0:
            xs = np.concatenate([np.zeros((HALO, D), dtype=np.float32),
                                 x[b, 0:s0 + NQ]], axis=0)
        else:
            xs = x[b, lo:s0 + NQ]
        cos_f, sin_f = _rope_tables_host()
        m = dict(shared)
        m["xT"] = np.ascontiguousarray(xs.T)
        m["cosin"] = np.ascontiguousarray(np.tile(cos_f[s0:s0 + NH].T, (2, 1)))
        m["sinin"] = np.ascontiguousarray(np.tile(sin_f[s0:s0 + NH].T, (2, 1)))
        m["mask01"] = _mask_for_core(chunk)
        in_maps.append(m)
    res = run_bass_kernel_spmd(nc, in_maps, core_ids=list(range(8)))
    out = np.empty((B, L, D), dtype=np.float32)
    for c in range(8):
        b, chunk = divmod(c, 4)
        s0 = chunk * NQ
        out[b, s0:s0 + NQ, :] = np.asarray(res.results[c]["o"]).T
    return out


# version 39 (final)
# speedup vs baseline: 1.0179x; 1.0179x over previous
import numpy as np
from contextlib import ExitStack

import concourse.bass as bass
import concourse.mybir as mybir
from concourse.bass_utils import run_bass_kernel_spmd

F32 = mybir.dt.float32
F16 = mybir.dt.float16
U8 = mybir.dt.uint8
AF = mybir.ActivationFunctionType
ALU = mybir.AluOpType
AX = mybir.AxisListType

D = 2048
NCH = 16
NQ = 1024
HALO = 512
NH = NQ + HALO
H = 16
WIN = 640
NT = 8
NMT = 5
SCALE = float(1.0 / np.sqrt(128.0))
_S128 = 1.0 / np.sqrt(np.float64(128.0))
SCALE_HI = float(np.float32(_S128))
SCALE_LO = float(_S128 - np.float64(SCALE_HI))
LORA_S = 1.0 / 16.0
PI = float(np.pi)
TWO_PI = float(2.0 * np.pi)

ENGINES = ("sync", "gpsimd", "tensor", "vector", "scalar")


class Prog:
    def __init__(self):
        self.ops = {e: [] for e in ENGINES}
        self.cnt = {}
        self.tok = {}

    def add(self, eng, fn, wait=(), inc=1, token=None, sem=None):
        sem = sem or eng
        waits = []
        for w in wait:
            assert w in self.tok, f"token {w!r} referenced before defined"
            waits.append(self.tok[w])
        self.cnt[sem] = self.cnt.get(sem, 0) + inc
        if token is not None:
            assert token not in self.tok, f"dup token {token}"
            self.tok[token] = (sem, self.cnt[sem])
        self.ops[eng].append((waits, fn, inc, sem))

    def alias(self, new, old):
        self.tok[new] = self.tok[old]

    def emit(self, block, sems):
        for eng in ENGINES:
            ops = self.ops[eng]
            if not ops:
                continue
            serial = eng in ("vector", "scalar")
            def body(h, eng=eng, ops=ops, serial=serial):
                hi = {}
                own = {s for (_, _, i, s) in ops if i > 0}
                cnt = {}
                for waits, fn, inc, sem in ops:
                    for (wsem, val) in waits:
                        if wsem in own:
                            continue
                        if val > hi.get(wsem, 0):
                            h.wait_ge(sems[wsem], val)
                            hi[wsem] = val
                    if (serial or sem == "gp_c") and inc > 0 and cnt.get(sem, 0) > 0:
                        h.wait_ge(sems[sem], cnt[sem])
                    inst = fn(h)
                    if inst is not None and inc > 0:
                        inst.then_inc(sems[sem], inc)
                        cnt[sem] = cnt.get(sem, 0) + inc
            getattr(block, eng)(body)


def build_nc():
    nc = bass.Bass("TRN2", target_bir_lowering=False, debug=False)

    xT = nc.dram_tensor("xT", [D, NH], F32, kind="ExternalInput")
    w_q = nc.dram_tensor("w_q", [16, 16, 128, 128], F32, kind="ExternalInput")
    w_k = nc.dram_tensor("w_k", [16, 16, 128, 128], F32, kind="ExternalInput")
    w_v = nc.dram_tensor("w_v", [16, 16, 128, 128], F32, kind="ExternalInput")
    w_o = nc.dram_tensor("w_o", [16, 16, 128, 128], F32, kind="ExternalInput")
    w_f = nc.dram_tensor("w_f", [16, 16, 128, 128], F32, kind="ExternalInput")
    a_qkv = nc.dram_tensor("a_qkv", [128, 1280], F32, kind="ExternalInput")
    a_o = nc.dram_tensor("a_o", [16, 128, 16], F32, kind="ExternalInput")
    b_qkv = nc.dram_tensor("b_qkv", [80, 2048], F32, kind="ExternalInput")
    b_o = nc.dram_tensor("b_o", [16, 2048], F32, kind="ExternalInput")
    a_f = nc.dram_tensor("a_f", [16, 128, 128], F32, kind="ExternalInput")
    b_f = nc.dram_tensor("b_f", [128, 2048], F32, kind="ExternalInput")
    g_w = nc.dram_tensor("g_w", [16, 128, 8], F32, kind="ExternalInput")
    g_b = nc.dram_tensor("g_b", [1, 8], F32, kind="ExternalInput")
    f_b = nc.dram_tensor("f_b", [1, 2048], F32, kind="ExternalInput")
    nw_d = nc.dram_tensor("nw", [128, 16], F32, kind="ExternalInput")
    cos_d = nc.dram_tensor("cosin", [128, NH], F32, kind="ExternalInput")
    sin_d = nc.dram_tensor("sinin", [128, NH], F32, kind="ExternalInput")
    mask_d = nc.dram_tensor("mask01", [128, NMT * WIN], F16, kind="ExternalInput")
    id_d = nc.dram_tensor("ident", [128, 128], F32, kind="ExternalInput")
    et_d = nc.dram_tensor("et", [8, 128], F16, kind="ExternalInput")
    sw_d = nc.dram_tensor("sw", [128, 128], F16, kind="ExternalInput")
    o_d = nc.dram_tensor("o", [D, NQ], F32, kind="ExternalOutput")

    P = Prog()
    st = ExitStack()

    def sb(name, shape, dt):
        return st.enter_context(nc.sbuf_tensor(name, shape, dt))

    ps0 = st.enter_context(nc.psum_tensor("ps0", [128, 512], F32))
    psS0 = st.enter_context(nc.psum_tensor("psS0", [128, 1024], F32))
    psS1 = st.enter_context(nc.psum_tensor("psS1", [128, 1024], F32))
    psT0 = st.enter_context(nc.psum_tensor("psT0", [128, 1024], F16))
    psP = st.enter_context(nc.psum_tensor("psP", [128, 512], F32))
    psO = st.enter_context(nc.psum_tensor("psO", [128, 512], F32))

    ASZ = 137376
    arena = st.enter_context(nc.sbuf_tensor("arena", [128, ASZ], U8))
    abase = nc.lookup_mloc(arena).addr

    def carve(name, shape, dtype, off):
        nbytes = int(np.prod(shape[1:])) * mybir.dt.size(dtype)
        assert off % 32 == 0 and off + nbytes <= ASZ, (name, off, nbytes)
        hnd = nc.virtual_sbuf_tensor(name, shape, dtype)
        nc._tensor(hnd.name, hnd.shape, hnd.dtype, type="SB")
        mloc = nc.lookup_mloc(hnd)
        mloc.allocated = True
        mloc.addr = abase + off
        return hnd

    h_sb = carve("h", [128, NCH * NH], F16, 0)
    cosT = carve("cosT", [128, NH], F16, 49152)
    sinT = carve("sinT", [128, NH], F16, 52224)
    maskb = carve("maskb", [128, NMT * WIN], F16, 55296)
    k16 = carve("k16", [128, NH], F16, 61696)
    v16 = carve("v16", [128, NH], F16, 64768)
    q16 = carve("q16", [128, NQ], F16, 67840)
    rt1 = carve("rt1", [128, NH], F16, 69888)
    rt2 = carve("rt2", [128, NH], F16, 72960)
    qro = carve("qro", [128, NQ], F16, 76032)
    kro = carve("kro", [128, NH], F16, 78080)
    vtok = carve("vtok", [128, NH], F16, 81152)
    p_sb = [carve(f"p_sb{i}", [128, WIN], F16, 84224 + i * 1280) for i in range(2)]
    pm_sb = [carve(f"pm_sb{i}", [128, WIN], F16, 86784 + i * 1280) for i in range(2)] + \
            [carve(f"pm_sb{i}", [128, WIN], F16, 120768 + (i - 2) * 1280) for i in range(2, 4)]
    pn_sb = [carve(f"pn_sb{i}", [128, WIN], F16, 89344 + i * 1280) for i in range(2)] + \
            [carve(f"pn_sb{i}", [128, WIN], F16, 123328 + (i - 2) * 1280) for i in range(2, 4)]
    pt_sb = [carve(f"pt_sb{i}", [128, WIN], F16, 91904 + i * 1280) for i in range(2)]
    rmin = [carve(f"rmin{i}", [128, 1], F32, 94464 + i * 32) for i in range(2)]
    rsum = [carve(f"rsum{i}", [128, 1], F32, 94464 + 64 + i * 32) for i in range(2)]
    rrec = [carve(f"rrec{i}", [128, 1], F32, 128960 + i * 32) for i in range(4)]
    bqkv = carve("bqkv", [80, 2048], F16, 94656)
    xa_all = carve("xa_all", [80, NH], F16, 106944)
    srow = carve("srow", [1, NH], F32, 84224)
    rrow = carve("rrow", [1, NH], F32, 90368)
    sres = carve("sres", [128, NCH * NQ], F16, 0)
    yf16 = carve("yf16", [128, NCH * NQ], F16, 32768)
    tmprow = carve("tmprow", [1, 1024], F32, 65536)
    srow2 = carve("srow2", [1, 1024], F32, 69632)
    mrow = carve("mrow", [1, 1024], F32, 73728)
    vrow = carve("vrow", [1, 1024], F32, 77824)
    tchunk = [carve(f"tchunk{i}", [128, NQ], F16, 96512 + i * 2048) for i in range(2)]
    s2f16 = [carve(f"s2f16{i}", [128, NQ], F16, 104704 + i * 2048) for i in range(2)]
    afb = carve("afb", [128, 2048], F16, 113088)
    aqkv_sb = carve("aqkv", [128, 1280], F16, 110016)
    aob = carve("aob", [128, 256], F16, 114624)
    swp = carve("swp", [128, NH], F16, 115136)
    sneg = [carve(f"sneg{i}", [128, WIN], F16, 118208 + i * 1280) for i in range(2)]
    swk = carve("swk", [128, NH], F16, 125888)
    bfb2 = carve("bfb2", [128, 2048], F16, 117184)
    logit = carve("logit", [8, 1024], F32, 121280)
    fbb = carve("fbb", [1, 2048], F16, 125376)
    gwb = carve("gwb", [128, 128], F16, 129472)
    logT = carve("logT", [128, 64], F32, 129728)
    gm1 = carve("gm1", [128, 8], F32, 129984)
    gtmp = carve("gtmp", [128, 64], F32, 130016)
    gind = carve("gind", [128, 64], F32, 130272)
    ge_ = carve("ge", [128, 64], F32, 130528)
    gew = carve("gew", [128, 64], F32, 130784)
    gden = carve("gden", [128, 8], F32, 131040)
    grec = carve("grec", [128, 8], F32, 131072)
    gw16 = carve("gw16", [128, 64], F16, 131104)
    wT_sb = carve("wT", [8, 1024], F16, 131232)
    xaf = carve("xaf", [128, 1024], F16, 133280)
    xaw = carve("xaw", [128, 1024], F16, 135328)

    id8 = sb("id8", [8, 8], F32)
    id16 = sb("id16", [128, 128], F16)
    etb = sb("etb", [8, 128], F16)
    nwb = sb("nwb", [128, 16], F32)
    onecol = sb("onecol", [128, 1], F16)
    onerow16 = sb("onerow16", [1, 512], F16)
    onerow32 = sb("onerow32", [1, 128], F32)
    swb = sb("swb", [128, 128], F16)
    b1e6 = sb("b1e6", [128, 1], F32)
    b1e5 = sb("b1e5", [128, 1], F32)
    gbb = sb("gbb", [1, 8], F16)
    NWC = 5
    wcol = [sb(f"wcol{i}", [128, 2048], F16) for i in range(NWC)]
    xbuf = [sb(f"xbuf{i}", [128, NH], F32) for i in range(3)]
    aoutT = sb("aoutT", [128, NCH * NQ], F16)
    bob = carve("bob", [16, 2048], F16, 98752)
    xa_q = carve("xa_q", [16, NH], F16, 110016)
    ystage = [xbuf[0], xbuf[1]]

    P.add("gpsimd", lambda h: h.dma_start(nwb[:], nw_d[:, :]), inc=16, sem="const")
    P.add("gpsimd", lambda h: h.dma_start(id16[:], id_d[:, :]), inc=16, sem="const")
    P.add("gpsimd", lambda h: h.dma_start(swb[:], sw_d[:, :]), inc=16, sem="const")
    P.add("gpsimd", lambda h: h.dma_start(aqkv_sb[:], a_qkv[:, :]),
          inc=16, token="ld_aqkv", sem="const")
    P.add("gpsimd", lambda h: h.dma_start(
        aob[:].rearrange("p (c r) -> p c r", r=16), a_o[:, :, :].rearrange("c p r -> p c r")),
        inc=16, token="ld_ao", sem="const")
    P.add("gpsimd", lambda h: h.dma_start(gbb[:], g_b[:, :]), inc=16, sem="const")

    P.add("gpsimd", lambda h: h.dma_start(
        gwb[:].rearrange("p (c g) -> p c g", g=8), g_w[:, :, :].rearrange("c p g -> p c g")),
        inc=16, sem="const")

    P.add("vector", lambda h: h.memset(onecol[:], 1.0), token="ones_c")
    P.add("vector", lambda h: h.memset(b1e6[:], 1e-6), token="c_1e6")
    P.add("vector", lambda h: h.memset(b1e5[:], 1e-5), token="c_1e5")
    P.add("vector", lambda h: h.memset(onerow16[:], 1.0), token="ones_r16")
    P.add("vector", lambda h: h.memset(onerow32[:], 1.0), token="ones_r32")

    P.add("gpsimd", lambda h: h.dma_start(cosT[:], cos_d[:, :]), inc=16, sem="const")
    P.add("gpsimd", lambda h: h.dma_start(sinT[:], sin_d[:, :]), inc=16, token="consts", sem="const")

    sqbuf = [rt2, kro]
    xqs = ("sync", "gpsimd", "sync")
    for ci in range(3):
        P.add(xqs[ci % 3], lambda h, ci=ci: h.dma_start(
            xbuf[ci % 3][:], xT[ci * 128:(ci + 1) * 128, :]),
            inc=16, token=f"ldx1_{ci}", sem=f"xb{ci % 3}")
    for ci in range(NCH):
        if ci >= 3:
            P.add(xqs[ci % 3], lambda h, ci=ci: h.dma_start(
                xbuf[ci % 3][:], xT[ci * 128:(ci + 1) * 128, :]),
                wait=(f"xc_{ci - 3}",), inc=16, token=f"ldx1_{ci}", sem=f"xb{ci % 3}")
        P.add("scalar", lambda h, ci=ci: h.copy(
            h_sb[:, ci * NH:(ci + 1) * NH], xbuf[ci % 3][:]),
            wait=(f"ldx1_{ci}",), token=f"xc_{ci}")
        wt = (f"xc_{ci}",) + ((f"x1_done{ci - 2}",) if ci >= 2 else ())
        P.add("vector", lambda h, ci=ci: h.tensor_tensor(
            sqbuf[ci % 2][:], h_sb[:, ci * NH:(ci + 1) * NH],
            h_sb[:, ci * NH:(ci + 1) * NH], op=ALU.mult),
            wait=wt, token=f"xsq_{ci}")
        def mmsq(h, ci=ci):
            last = None
            for p in range(3):
                dst = [ps0[0:1, :], psS0[0:1, 0:512], psS0[0:1, 512:1024]][p]
                last = h.matmul(dst, onecol[:], sqbuf[ci % 2][:, p * 512:(p + 1) * 512],
                                start=(ci == 0), stop=(ci == NCH - 1))
            return last
        P.add("tensor", mmsq, wait=(f"xsq_{ci}", "ones_c"), token=f"x1_done{ci}")
    P.add("sync", lambda h: h.dma_start(maskb[:], mask_d[:, :]), inc=16, sem="const_s")
    P.add("sync", lambda h: h.dma_start(id8[:], id_d[0:8, 0:8]), inc=16, sem="const_s")
    P.add("sync", lambda h: h.dma_start(etb[:], et_d[:, :]), inc=16, sem="const_s", token="consts_s")
    P.add("scalar", lambda h: h.copy(srow[0:1, 0:512], ps0[0:1, :]),
          wait=(f"x1_done{NCH - 1}",), token="ms_e0")
    P.add("scalar", lambda h: h.copy(srow[0:1, 512:1024], psS0[0:1, 0:512]), token="ms_e1")
    P.add("scalar", lambda h: h.copy(srow[0:1, 1024:1536], psS0[0:1, 512:1024]), token="ms_e2")
    P.add("scalar", lambda h: h.activation(rrow[:], srow[:], AF.Ln, bias=b1e6[0:1, :], scale=1.0 / 2048.0),
          wait=("ms_e2", "c_1e6"), token="rms_ln")
    P.add("scalar", lambda h: h.activation(rrow[:], rrow[:], AF.Exp, scale=-0.5),
          token="rms_rstd")
    def bc_rms(h):
        last = None
        for p in range(3):
            dst = [psS1[:, 0:512], psS1[:, 512:1024], psP[:, 0:512]][p]
            last = h.matmul(dst, onerow32[:], rrow[0:1, p * 512:(p + 1) * 512],
                            start=True, stop=True)
        return last
    P.add("tensor", bc_rms, wait=("rms_rstd", "ones_r32"), token="rms_bc")
    for pp in range(3):
        srcb = [psS1[:, 0:512], psS1[:, 512:1024], psP[:, 0:512]][pp]
        P.add("scalar", lambda h, pp=pp, srcb=srcb: h.copy(
            vtok[:, pp * 512:(pp + 1) * 512], srcb),
            wait=(("rms_bc",) if pp == 0 else ()), token=f"rstd16_{pp}")

    for nm, dst, src in [("bqkv", bqkv, b_qkv), ("bo", bob, b_o)]:
        P.add("gpsimd", lambda h, d=dst, s=src: h.dma_start(d[:], s[:, :]),
              wait=("rms_bc",), inc=16, token=f"ld_{nm}", sem="bload")
    P.alias("blds", "ld_bo")

    for cb in range(0, NCH, 2):
        wts = ["rstd16_2", "consts"]
        def happ(h, cb=cb):
            last = None
            for ci in (cb, cb + 1):
                for p in range(3):
                    sl = h_sb[:, ci * NH + p * 512: ci * NH + (p + 1) * 512]
                    last = h.scalar_tensor_tensor(
                        sl, sl, nwb[:, ci:ci + 1], vtok[:, p * 512:(p + 1) * 512],
                        op0=ALU.mult, op1=ALU.mult)
            return last
        P.add("vector", happ, wait=tuple(wts), token=f"h_{cb + 1}")
        P.alias(f"h_{cb}", f"h_{cb + 1}")

    for col in range(NWC):
        hh0, j0 = divmod(col, 3)
        wsrc = (w_q, w_k, w_v)[j0]
        P.add("gpsimd", lambda h, b=col % NWC, w=wsrc, mo=hh0: h.dma_start(
            wcol[b][:].rearrange("p (c j) -> p c j", j=128),
            w[mo].rearrange("c p j -> p c j")),
            wait=(("ldx1_15",) if col == 0 else ()), inc=16, token=f"wc_{hh0}_{j0}",
            sem=f"wcs{col % NWC}")

    xabank = [ps0[0:80, :], psS0[0:80, 0:512], psS0[0:80, 512:1024]]
    h0pre = [(0, 0, HALO, psP[:, :], "rstd16_2"),
             (0, 1, HALO, psS1[:, 0:512], "rstd16_2"),
             (1, 0, 0, psS1[:, 512:1024], "rstd16_2"),
             (1, 1, 0, psO[:, :], None)]
    for ci in range(NCH):
        for p in range(3):
            def mmxa(h, p=p, ci=ci):
                return h.matmul(xabank[p], aqkv_sb[:, ci * 80:(ci + 1) * 80],
                                h_sb[:, ci * NH + p * 512: ci * NH + (p + 1) * 512],
                                start=(ci == 0), stop=(ci == NCH - 1))
            wt = [f"h_{ci}"]
            if ci == 0 and p == 0:
                wt += ["consts_s", "consts"]
            P.add("tensor", mmxa, wait=tuple(wt),
                  token=(f"xa_m_{p}" if ci == NCH - 1 else None))
        for j, pp, off, pbk, guard in h0pre:
            def mmd(h, j=j, pp=pp, off=off, pbk=pbk, ci=ci):
                return h.matmul(
                    pbk, wcol[j][:, ci * 128:(ci + 1) * 128],
                    h_sb[:, ci * NH + off + pp * 512: ci * NH + off + (pp + 1) * 512],
                    start=(ci == 0), stop=False)
            wt = [f"h_{ci}"]
            if ci == 0:
                wt.append(f"wc_0_{j}")
                if guard is not None:
                    wt.append(guard)
            P.add("tensor", mmd, wait=tuple(wt))
    for p in range(3):
        P.add("scalar", lambda h, p=p: h.activation(
            xa_all[:, p * 512:(p + 1) * 512], xabank[p], AF.Copy, scale=LORA_S),
            wait=(f"xa_m_{p}",), token=f"xa_e_{p}")

    def mk_attn_items(hh):
        def item_sc(ti):
            def go():
                par = ti % 2
                psS = (psS0, psS1)[par]
                mt = min(ti, 4)
                def sc(h, ti=ti, psS=psS):
                    h.matmul(psS[:, 0:512], qro[:, ti * 128:(ti + 1) * 128],
                             kro[:, ti * 128: ti * 128 + 512], start=True, stop=True)
                    return h.matmul(psS[:, 512:640], qro[:, ti * 128:(ti + 1) * 128],
                                    kro[:, ti * 128 + 512: ti * 128 + 640],
                                    start=True, stop=True)
                waits = [f"ropeq_{hh}", f"ropek_{hh}"]
                if ti >= 2:
                    waits.append(f"ts_{hh}_{ti - 2}")
                    waits.append(f"exp_{hh}_{ti - 2}")
                elif hh > 0:
                    waits.append(f"ts_{hh - 1}_{6 + ti}")
                    waits.append(f"exp_{hh - 1}_{6 + ti}")
                elif ti == 0:
                    waits.append("xa_e_2")
                if ti == 0:
                    waits.append(f"vtr_e_{hh}_0")
                P.add("tensor", sc, wait=tuple(waits), token=f"sc_{hh}_{ti}")
                P.add("vector", lambda h, psS=psS, par=par: h.tensor_scalar(
                    sneg[par][:], psS[:, 0:WIN], -1.0, 3.0e38, op0=ALU.mult, op1=ALU.min,
                    accum_out=rmin[par][:]),
                    wait=(f"sc_{hh}_{ti}",), token=f"ts_{hh}_{ti}")
                P.add("scalar", lambda h, par=par: h.activation(
                    p_sb[par][:], sneg[par][:], AF.Exp, bias=rmin[par][:], scale=-1.0),
                    wait=(f"ts_{hh}_{ti}",), token=f"exp_{hh}_{ti}")
                wpm = [f"exp_{hh}_{ti}", "consts_s", "consts"]
                if ti >= 4:
                    wpm.append(f"pn_{hh}_{ti - 4}")
                elif hh > 0:
                    wpm.append(f"pn_{hh - 1}_{4 + ti}")
                P.add("vector", lambda h, mt=mt, par=par, ti=ti: h.scalar_tensor_tensor(
                    pm_sb[ti % 4][:], p_sb[par][:], 1.0, maskb[:, mt * WIN:(mt + 1) * WIN],
                    op0=ALU.mult, op1=ALU.mult, accum_out=rsum[par][:]),
                    wait=tuple(wpm), token=f"pm_{hh}_{ti}")
                P.add("vector", lambda h: h.drain(), inc=0)
                wrc = []
                if ti >= 4:
                    wrc.append(f"pn_{hh}_{ti - 4}")
                elif hh > 0:
                    wrc.append(f"pn_{hh - 1}_{4 + ti}")
                P.add("vector", lambda h, par=par, ti=ti: h.reciprocal(
                    rrec[ti % 4][:], rsum[par][:]),
                      wait=tuple(wrc), token=f"rc_{hh}_{ti}")
                emit_pn(ti)
            return go

        def emit_pn(ti):
            wpn = [f"rc_{hh}_{ti}"]
            if ti >= 4:
                wpn.append(f"ptr_{hh}_{ti - 4}")
            elif hh > 0:
                wpn.append(f"ptr_{hh - 1}_{4 + ti}")
            P.add("vector", lambda h, ti=ti: h.tensor_scalar(
                pn_sb[ti % 4][:], pm_sb[ti % 4][:], rrec[ti % 4][:], None, op0=ALU.mult),
                wait=tuple(wpn), token=f"pn_{hh}_{ti}")

        def item_ptr(ti):
            def go():
                par = ti % 2
                def ptr(h, ti=ti):
                    last = None
                    for s in range(5):
                        last = h.transpose(psT0[:, s * 128:(s + 1) * 128],
                                           pn_sb[ti % 4][:, s * 128:(s + 1) * 128], id16[:])
                    return last
                waits = [f"pn_{hh}_{ti}"]
                if ti >= 1:
                    waits.append(f"ptevac_{hh}_{ti - 1}")
                else:
                    waits.append(f"vtr_e_{hh}_2")
                P.add("tensor", ptr, wait=tuple(waits), token=f"ptr_{hh}_{ti}")
                P.add("scalar", lambda h, par=par: h.copy(pt_sb[par][:],
                                                          psT0[:, 0:WIN]),
                      wait=(f"ptr_{hh}_{ti}",), token=f"ptevac_{hh}_{ti}")
            return go

        def item_av(ti):
            def go():
                par = ti % 2
                def av(h, ti=ti, par=par):
                    last = None
                    for kt in range(5):
                        last = h.matmul(psO[:, (ti % 4) * 128:(ti % 4 + 1) * 128],
                                        vtok[:, (ti + kt) * 128:(ti + kt + 1) * 128],
                                        pt_sb[par][:, kt * 128:(kt + 1) * 128],
                                        start=(kt == 0), stop=(kt == 4))
                    return last
                waits = [f"ptevac_{hh}_{ti}", f"vtr_e_{hh}_2"]
                if ti == 0 and hh > 0:
                    waits.append(f"oevac_{hh - 1}_1")
                elif ti == 4:
                    waits.append(f"oevac_{hh}_0")
                P.add("tensor", av, wait=tuple(waits), token=f"av_{hh}_{ti}")
                if ti % 4 == 3:
                    P.add("scalar", lambda h, ti=ti: h.copy(
                        aoutT[:, hh * NQ + (ti - 3) * 128: hh * NQ + (ti + 1) * 128],
                        psO[:, :]),
                        wait=(f"av_{hh}_{ti}",), token=f"oevac_{hh}_{ti // 4}")
            return go

        return [item_sc(0), item_sc(1), item_sc(2), item_ptr(0), item_sc(3),
                [item_av(0), item_ptr(1)], item_sc(4), [item_av(1), item_ptr(2)],
                item_sc(5), [item_av(2), item_ptr(3)], item_sc(6),
                [item_av(3), item_ptr(4)], item_sc(7), [item_av(4), item_ptr(5)],
                [item_av(5), item_ptr(6)], [item_av(6), item_ptr(7), item_av(7)]]

    prev_items = []
    tail_q = []
    vtr_q = []

    def run_item(it):
        if isinstance(it, list):
            for f in it:
                f()
        else:
            it()

    for hh in range(H):
        if hh == 1:
            P.add("gpsimd", lambda h: h.dma_start(
                wcol[0][:].rearrange("p (c j) -> p c j", j=128),
                w_v[1].rearrange("c p j -> p c j")),
                wait=("proj_0_0_done",), inc=16, token="wc_1_2", sem="wcs0")
        if hh >= 2:
            for j, wsrc in enumerate((w_q, w_k, w_v)):
                col = 3 * hh + j
                bi = col % NWC
                pcol = col - NWC
                P.add("gpsimd", lambda h, b=bi, w=wsrc, mo=hh: h.dma_start(
                    wcol[b][:].rearrange("p (c j) -> p c j", j=128),
                    w[mo].rearrange("c p j -> p c j")),
                    wait=(f"proj_{pcol // 3}_{pcol % 3}_done",), inc=16,
                    token=f"wc_{hh}_{j}", sem=f"wcs{bi}")

        gi_box = [0]

        def bank_guard(gi):
            if hh == 0:
                if gi == 4:
                    return "xa_e_0"
                if gi == 5:
                    return "pe_evac_0_0"
            if gi >= 2:
                return f"pe_evac_{hh}_{gi - 2}"
            if hh > 0:
                return f"pe_evac_{hh - 1}_{6 if gi == 0 else 12}"
            return "xa_e_0" if gi == 0 else "rstd16_2"
        pbank = [ps0, psP]
        if hh > 0:
            swap_banks = [(psS0[:, 0:512], f"ts_{hh - 1}_6"),
                          (psS0[:, 512:1024], f"ts_{hh - 1}_6"),
                          (psS1[:, 0:512], f"ts_{hh - 1}_7"),
                          (psS1[:, 512:1024], f"ts_{hh - 1}_7"),
                          (psP[:, :], f"pe_evac_{hh}_7")]
        else:
            swap_banks = [(psS0[:, 0:512], "xa_e_2"), (psS0[:, 512:1024], "xa_e_2"),
                          (psS1[:, 0:512], "pe_evac_0_1"), (psS1[:, 512:1024], "pe_evac_0_2"),
                          (psP[:, :], f"pe_evac_{hh}_7")]

        def proj_item(name, j, wbi, p, off, r0, dst16, bov=None, pre=False):
            def go():
                gi = gi_box[0]
                bank = bov if bov is not None else pbank[gi % 2][:, :]
                def mmp(h, bank=bank, wbi=wbi, p=p, off=off, r0=r0, hh=hh, pre=pre):
                    last = None
                    if not pre:
                        for ci in range(NCH):
                            last = h.matmul(
                                bank, wcol[wbi][:, ci * 128:(ci + 1) * 128],
                                h_sb[:, ci * NH + off + p * 512:
                                     ci * NH + off + (p + 1) * 512],
                                start=(ci == 0), stop=False)
                    return h.matmul(bank, bqkv[r0:r0 + 16, hh * 128:(hh + 1) * 128],
                                    xa_all[r0:r0 + 16, off + p * 512: off + (p + 1) * 512],
                                    start=False, stop=True)
                wts = [f"wc_{hh}_{j}", "xa_e_2", "blds"]
                if not pre:
                    wts.append(bank_guard(gi))
                P.add("tensor", mmp, wait=tuple(wts),
                      token=f"pe_g_{hh}_{gi}")
                ewaits = [f"pe_g_{hh}_{gi}"]
                if hh > 0:
                    if name == "q" and p == 0:
                        ewaits.append(f"ropeq1_{hh - 1}")
                    if name == "k" and p == 0:
                        ewaits.append(f"ropek1_{hh - 1}")
                P.add("scalar", lambda h, dst16=dst16, p=p, bank=bank: h.copy(
                    dst16[:, p * 512:(p + 1) * 512], bank),
                    wait=tuple(ewaits), token=f"pe_evac_{hh}_{gi}")
                gi_box[0] += 1
            return go

        def swap_item(src16, p):
            def go():
                gi = gi_box[0]
                bank, sguard = swap_banks[gi - 8]
                def mms(h, bank=bank, src16=src16, p=p):
                    return h.matmul(bank, swb[:], src16[:, p * 512:(p + 1) * 512],
                                    start=True, stop=True)
                P.add("tensor", mms, wait=("consts", sguard),
                      token=f"pe_g_{hh}_{gi}")
                ewaits = [f"pe_g_{hh}_{gi}"]
                dst = rt2 if src16 is q16 else swk
                if hh > 0 and p == 0:
                    ewaits.append(f"ropeq2_{hh - 1}" if src16 is q16 else
                                  f"ropek2_{hh - 1}")
                P.add("scalar", lambda h, p=p, dst=dst, bank=bank: h.copy(
                    dst[:, p * 512:(p + 1) * 512], bank),
                    wait=tuple(ewaits), token=f"pe_evac_{hh}_{gi}")
                gi_box[0] += 1
            return go

        def vtr_item(b, pst_free, hh=hh):
            vbank = psS0[:, 0:512] if b == 0 else psT0[:, 0:512]
            def go():
                def vtr(h, b=b, vbank=vbank):
                    last = None
                    for s in range(4):
                        kt = b * 4 + s
                        if b == 0:
                            last = h.matmul(vbank[:, s * 128:(s + 1) * 128],
                                            v16[:, kt * 128:(kt + 1) * 128], id16[:],
                                            start=True, stop=True)
                        else:
                            last = h.transpose(vbank[:, s * 128:(s + 1) * 128],
                                               v16[:, kt * 128:(kt + 1) * 128], id16[:])
                    return last
                waits = [f"pe_evac_{hh}_7", "consts_s", "consts"]
                if b == 0:
                    waits.append(f"pe_evac_{hh}_8")
                elif b == 2:
                    waits.append(f"vtr_e_{hh}_1")
                if pst_free is not None:
                    waits.append(pst_free)
                P.add("tensor", vtr, wait=tuple(waits), token=f"vtr_{hh}_{b}")
                P.add("scalar", lambda h, b=b, vbank=vbank: h.copy(
                    vtok[:, b * 512:(b + 1) * 512], vbank),
                    wait=(f"vtr_{hh}_{b}",), token=f"vtr_e_{hh}_{b}")
            return go

        def rope_q():
            wq1 = [f"pe_evac_{hh}_1", "consts_s", "consts"]
            if hh > 0:
                wq1.append(f"sc_{hh - 1}_7")
            P.add("vector", lambda h: h.scalar_tensor_tensor(
                rt1[:, 0:NQ], q16[:], SCALE, cosT[:, HALO:NH],
                op0=ALU.mult, op1=ALU.mult),
                  wait=tuple(wq1), token=f"ropeq1_{hh}")
            P.add("vector", lambda h: h.scalar_tensor_tensor(
                swp[:, 0:NQ], rt2[:, 0:NQ], SCALE, sinT[:, HALO:NH],
                op0=ALU.mult, op1=ALU.mult),
                  wait=(f"pe_evac_{hh}_9",), token=f"ropeq2_{hh}")
            P.add("vector", lambda h: h.tensor_tensor(qro[:], rt1[:, 0:NQ], swp[:, 0:NQ],
                                                      op=ALU.add), token=f"ropeq_{hh}")

        def rope_k():
            wk1 = [f"pe_evac_{hh}_4", f"ropeq_{hh}"]
            if hh == 0:
                wk1.append(f"x1_done{NCH - 1}")
            elif hh > 0:
                wk1.append(f"sc_{hh - 1}_7")
            P.add("vector", lambda h: h.tensor_tensor(rt1[:], k16[:], cosT[:], op=ALU.mult),
                  wait=tuple(wk1), token=f"ropek1_{hh}")
            P.add("vector", lambda h: h.tensor_tensor(swp[:], swk[:], sinT[:], op=ALU.mult),
                  wait=(f"pe_evac_{hh}_12",), token=f"ropek2_{hh}")
            P.add("vector", lambda h: h.tensor_tensor(kro[:], rt1[:], swp[:], op=ALU.add),
                  token=f"ropek_{hh}")

        if hh == 0:
            seq = (
                [proj_item("q", 0, 0, p, HALO, 0, q16,
                           bov=h0pre[p][3], pre=True) for p in range(2)]
                + [proj_item("k", 1, 1, p, 0, 32, k16,
                             bov=h0pre[2 + p][3], pre=True) for p in range(2)]
                + [proj_item("k", 1, 1, 2, 0, 32, k16)]
                + [proj_item("v", 2, 2, p, 0, 64, v16) for p in range(3)]
                + [swap_item(q16, p) for p in range(2)]
                + [swap_item(k16, p) for p in range(3)]
            )
        else:
            seq = (
                [proj_item("q", 0, (3 * hh + 0) % NWC, p, HALO, 0, q16) for p in range(2)]
                + [proj_item("k", 1, (3 * hh + 1) % NWC, p, 0, 32, k16) for p in range(3)]
                + [proj_item("v", 2, (3 * hh + 2) % NWC, p, 0, 64, v16) for p in range(3)]
                + [swap_item(q16, p) for p in range(2)]
                + [swap_item(k16, p) for p in range(3)]
            )
        seq[0]()
        if vtr_q:
            vtr_q[0]()
        if tail_q:
            run_item(tail_q[0])
            run_item(tail_q[1])
            run_item(tail_q[2])
        if vtr_q:
            vtr_q[1]()
        pair_at = {1: (0,), 2: (1, 2), 3: (3, 4), 4: (5, 6), 5: (7, 8),
                   6: (9, 10), 7: (11, 12)}
        for i, g in enumerate(seq[1:], start=1):
            g()
            if prev_items:
                for j in pair_at.get(i, ()):
                    run_item(prev_items[j])
            if i == 1 and vtr_q:
                vtr_q[2]()
        rope_q()
        rope_k()
        P.alias(f"proj_{hh}_0_done", f"pe_g_{hh}_1")
        P.alias(f"proj_{hh}_1_done", f"pe_g_{hh}_4")
        P.alias(f"proj_{hh}_2_done", f"pe_g_{hh}_7")
        items = mk_attn_items(hh)
        tail_q = prev_items[13:16] if len(prev_items) > 13 else []
        prev_items = items
        vtr_q = [vtr_item(0, None),
                 vtr_item(1, f"ptevac_{hh - 1}_7" if hh >= 1 else None),
                 vtr_item(2, None)]

    vtr_q[0]()
    run_item(tail_q[0])
    run_item(tail_q[1])
    run_item(tail_q[2])
    vtr_q[1]()
    vtr_q[2]()
    ep_items = list(prev_items)
    ep_pos = [0]

    def ep_pop(n=1):
        for _ in range(n):
            if ep_pos[0] < len(ep_items):
                run_item(ep_items[ep_pos[0]])
                ep_pos[0] += 1

    lnb = {(0, 0): psS1[0:1, 0:512], (0, 1): psS1[0:1, 512:1024],
           (1, 0): psP[0:1, 0:512], (1, 1): psO[0:1, 0:512]}

    def emit_ln_stat(tag, src_t, ci, extra=()):
        wts = [f"{tag}_src_{ci}"]
        if ci >= 2:
            wts.append(f"{tag}_st_{ci - 2}")
        P.add("vector", lambda h, ci=ci: h.tensor_tensor(
            s2f16[ci % 2][:], src_t[:, ci * NQ:(ci + 1) * NQ],
            src_t[:, ci * NQ:(ci + 1) * NQ], op=ALU.mult),
            wait=tuple(wts), token=f"{tag}_sq_{ci}")
        def mst(h, ci=ci):
            last = None
            for p in range(2):
                last = h.matmul(lnb[(0, p)], onecol[:],
                                src_t[:, ci * NQ + p * 512: ci * NQ + (p + 1) * 512],
                                start=(ci == 0), stop=(ci == NCH - 1))
                last = h.matmul(lnb[(1, p)], onecol[:],
                                s2f16[ci % 2][:, p * 512:(p + 1) * 512],
                                start=(ci == 0), stop=(ci == NCH - 1))
            return last
        P.add("tensor", mst, wait=(f"{tag}_sq_{ci}",) + tuple(extra),
              token=f"{tag}_st_{ci}")

    def emit_ln_finish(tag, src_t, dst16, dst32, stage_dma):
        banksF = {(0, 0): psS1[:, 0:512], (0, 1): psS1[:, 512:1024],
                  (1, 0): psP[:, :], (1, 1): psO[:, 0:512]}
        P.add("scalar", lambda h: h.activation(mrow[0:1, 0:1024], psS1[0:1, 0:1024],
                                               AF.Copy, scale=1.0 / 2048.0),
              wait=(f"{tag}_st_{NCH - 1}",), token=f"{tag}_rm")
        P.add("scalar", lambda h: h.activation(vrow[0:1, 0:512], lnb[(1, 0)],
                                               AF.Copy, scale=1.0 / 2048.0),
              token=f"{tag}_r2")
        P.add("scalar", lambda h: h.activation(vrow[0:1, 512:1024], lnb[(1, 1)],
                                               AF.Copy, scale=1.0 / 2048.0),
              token=f"{tag}_r3")
        P.add("vector", lambda h: h.tensor_tensor(tmprow[:], mrow[:], mrow[:], op=ALU.mult),
              wait=(f"{tag}_rm",), token=f"{tag}_m2")
        P.add("vector", lambda h: h.tensor_tensor(vrow[:], vrow[:], tmprow[:],
                                                  op=ALU.subtract),
              wait=(f"{tag}_r3",), token=f"{tag}_var")
        P.add("scalar", lambda h: h.activation(rrow[0:1, 0:1024], vrow[:], AF.Ln, bias=b1e5[0:1, :]),
              wait=(f"{tag}_var", "c_1e5"), token=f"{tag}_vln")
        P.add("scalar", lambda h: h.activation(rrow[0:1, 0:1024], rrow[0:1, 0:1024], AF.Exp,
                                               scale=-0.5), token=f"{tag}_rstd")
        P.add("vector", lambda h: h.tensor_tensor(mrow[:], mrow[:], rrow[0:1, 0:1024],
                                                  op=ALU.mult),
              wait=(f"{tag}_rstd",), token=f"{tag}_mr")
        def bcs(h):
            h.matmul(banksF[(0, 0)], onerow32[:], rrow[0:1, 0:512], start=True, stop=True)
            h.matmul(banksF[(0, 1)], onerow32[:], rrow[0:1, 512:1024], start=True, stop=True)
            h.matmul(banksF[(1, 0)], onerow32[:], mrow[0:1, 0:512], start=True, stop=True)
            return h.matmul(banksF[(1, 1)], onerow32[:], mrow[0:1, 512:1024],
                            start=True, stop=True)
        P.add("tensor", bcs, wait=(f"{tag}_mr",), token=f"{tag}_bc")
        P.add("scalar", lambda h: h.copy(rt1[:, 0:1024], psS1[:, 0:1024]),
              wait=(f"{tag}_bc",), token=f"{tag}_f0")
        P.add("scalar", lambda h: h.copy(rt2[:, 0:512], psP[:, :]), token=f"{tag}_f1")
        P.add("scalar", lambda h: h.copy(rt2[:, 512:1024], psO[:, 0:512]),
              token=f"{tag}_fb")

        def dst_of(cj):
            if dst16 is not None:
                return dst16, cj * NQ
            return dst32[cj % 2], 0

        for ci in range(NCH + 1):
            def app(h, ci=ci):
                last = None
                if ci < NCH:
                    for p in range(2):
                        last = h.scalar_tensor_tensor(
                            tchunk[ci % 2][:, p * 512:(p + 1) * 512],
                            src_t[:, ci * NQ + p * 512: ci * NQ + (p + 1) * 512], 1.0,
                            rt1[:, p * 512:(p + 1) * 512], op0=ALU.mult, op1=ALU.mult)
                if ci >= 1:
                    d, off = dst_of(ci - 1)
                    for p in range(2):
                        last = h.tensor_tensor(
                            d[:, off + p * 512: off + (p + 1) * 512],
                            tchunk[(ci - 1) % 2][:, p * 512:(p + 1) * 512],
                            rt2[:, p * 512:(p + 1) * 512], op=ALU.subtract)
                return last
            wapp = [f"{tag}_fb"] if ci == 0 else []
            if stage_dma and ci >= 3:
                wapp.append(f"out_{ci - 3}")
            P.add("vector", app, wait=tuple(wapp),
                  token=(f"{tag}_y_{ci - 1}" if ci >= 1 else None))
            if stage_dma and ci >= 1:
                cj = ci - 1
                P.add("sync", lambda h, cj=cj: h.dma_start(
                    o_d[cj * 128:(cj + 1) * 128, :], ystage[cj % 2][:, 0:NQ]),
                    wait=(f"{tag}_y_{cj}",), inc=16, token=f"out_{cj}",
                    sem=f"ob{cj % 2}")


    for p in range(2):
        bank = [ps0[0:16, :], psP[0:16, :]][p]
        for ci in range(NCH):
            def mmxo(h, p=p, bank=bank, ci=ci):
                return h.matmul(bank, aob[:, ci * 16:(ci + 1) * 16],
                                aoutT[:, ci * NQ + p * 512: ci * NQ + (p + 1) * 512],
                                start=(ci == 0), stop=(ci == NCH - 1))
            wts = [f"oevac_{ci}_{p}"]
            if ci == 0:
                wts += ["consts_s", "consts",
                        f"pe_evac_{H - 1}_{6 if p == 0 else 12}"]
            P.add("tensor", mmxo, wait=tuple(wts),
                  token=(f"xo_m_{p}" if ci == NCH - 1 else None))
            if p == 0 and ci < NCH - 1:
                ep_pop(1)
            if p == 0 and ci == NCH - 2:
                ep_pop(1)
        P.add("scalar", lambda h, p=p, bank=bank: h.activation(
            xa_q[0:16, p * 512:(p + 1) * 512], bank, AF.Copy, scale=LORA_S),
            wait=(f"xo_m_{p}",), token=f"xo_e_{p}")

    P.add("gpsimd", lambda h: h.dma_start(
        afb[:].rearrange("p (c e) -> p c e", e=128), a_f[:, :, :].rearrange("c p e -> p c e")),
        wait=("xo_m_1",), inc=16, token="ld_af", sem="af")
    P.add("gpsimd", lambda h: h.dma_start(bfb2[:], b_f[:, :]), wait=("xo_m_1",),
          inc=16, sem="af")
    P.add("gpsimd", lambda h: h.dma_start(fbb[:], f_b[:, :]), wait=("xo_m_1",),
          inc=16, token="ld_bf2", sem="af")
    def emit_woc(mo, waits):
        P.add("gpsimd", lambda h, b=mo % NWC, mo=mo: h.dma_start(
            wcol[b][:].rearrange("p (c j) -> p c j", j=128),
            w_o[mo].rearrange("c p j -> p c j")),
            wait=waits, inc=16, token=f"woc_{mo}", sem=f"wcs{mo % NWC}")

    def emit_ldx3(mo, waits):
        P.add("sync", lambda h, mo=mo: h.dma_start(
            xbuf[mo % 2][:, 0:NQ], xT[mo * 128:(mo + 1) * 128, HALO:NH]),
            wait=waits, inc=16, token=f"ldx3_{mo}", sem=("xb0", "xb2")[mo % 2])

    for mo in range(NWC):
        emit_woc(mo, (f"pe_evac_{H - 1}_7",))
    for mo in range(2):
        emit_ldx3(mo, (f"h_{NCH - 1}",))

    for mo in range(NCH):
        for p in range(2):
            bank = [ps0[:, :], psP[:, :]][p]
            for ci in range(NCH):
                def mmo(h, mo=mo, p=p, bank=bank, ci=ci):
                    return h.matmul(bank, wcol[mo % NWC][:, ci * 128:(ci + 1) * 128],
                                    aoutT[:, ci * NQ + p * 512: ci * NQ + (p + 1) * 512],
                                    start=(ci == 0), stop=False)
                wts = [f"oevac_{ci}_{p}"]
                if ci == 0:
                    wts += [f"woc_{mo}", "blds"]
                    gidx = mo * 2 + p
                    if gidx >= 2:
                        wts.append(f"sres_g_{gidx - 2}")
                    else:
                        wts.append(f"xo_e_{p}")
                P.add("tensor", mmo, wait=tuple(wts))
            def mmo_l(h, mo=mo, p=p, bank=bank):
                return h.matmul(bank, bob[:, mo * 128:(mo + 1) * 128],
                                xa_q[0:16, p * 512:(p + 1) * 512], start=False, stop=True)
            P.add("tensor", mmo_l, wait=(f"xo_e_{p}",), token=f"po_g_{mo}_{p}")
            P.add("vector", lambda h, mo=mo, p=p, bank=bank: h.scalar_tensor_tensor(
                sres[:, mo * NQ + p * 512: mo * NQ + (p + 1) * 512],
                xbuf[mo % 2][:, p * 512:(p + 1) * 512], 2.0, bank,
                op0=ALU.mult, op1=ALU.add),
                wait=(f"po_g_{mo}_{p}", f"ldx3_{mo}"), token=f"sres_g_{gidx}")
        P.alias(f"sres_{mo}", f"sres_g_{mo * 2 + 1}")
        P.alias(f"ln1_src_{mo}", f"sres_{mo}")
        if mo + NWC < NCH:
            emit_woc(mo + NWC, (f"po_g_{mo}_1",))
        if mo + 2 < NCH:
            emit_ldx3(mo + 2, (f"sres_{mo}",))

    for ci in range(NCH):
        emit_ln_stat("ln1", sres, ci,
                     extra=((f"ts_{H - 1}_7", "sres_g_31",
                             f"oevac_{H - 1}_1") if ci == 0 else ()))
    emit_ln_finish("ln1", sres, yf16, None, False)

    lgbank = psS0[0:8, 512:1024]
    for ci in range(NCH):
        def mmlg(h, ci=ci):
            return h.matmul(lgbank, gwb[:, ci * 8:(ci + 1) * 8],
                            yf16[:, ci * NQ: ci * NQ + 512],
                            start=(ci == 0), stop=False)
        wts = [f"ln1_y_{ci}"]
        if ci == 0:
            wts += ["consts_s", "consts", "ones_r16", "ld_bf2"]
        P.add("tensor", mmlg, wait=tuple(wts))
        for p in range(2):
            bank = [ps0[:, :], psS0[:, 0:512]][p]
            def mmxa(h, p=p, bank=bank, ci=ci):
                return h.matmul(bank, afb[:, ci * 128:(ci + 1) * 128],
                                yf16[:, ci * NQ + p * 512: ci * NQ + (p + 1) * 512],
                                start=(ci == 0), stop=(ci == NCH - 1))
            P.add("tensor", mmxa, wait=(f"ln1_y_{ci}",),
                  token=(f"xaf_m_{p}" if ci == NCH - 1 else None))
    for p in range(2):
        bank = [ps0[:, :], psS0[:, 0:512]][p]
        P.add("scalar", lambda h, p=p, bank=bank: h.copy(xaf[:, p * 512:(p + 1) * 512], bank),
              wait=(f"xaf_m_{p}",), token=f"xaf_e_{p}")
    P.add("tensor", lambda h: h.matmul(lgbank, gbb[:], onerow16[0:1, 0:512],
                                       start=False, stop=True), token="lg_0")
    P.add("scalar", lambda h: h.copy(logit[:, 0:512], lgbank),
          wait=("lg_0",), token="lg_e_0")
    for ci in range(NCH):
        def mmlg2(h, ci=ci):
            return h.matmul(lgbank, gwb[:, ci * 8:(ci + 1) * 8],
                            yf16[:, ci * NQ + 512: ci * NQ + 1024],
                            start=(ci == 0), stop=False)
        P.add("tensor", mmlg2, wait=(("lg_e_0",) if ci == 0 else ()))
    P.add("tensor", lambda h: h.matmul(lgbank, gbb[:], onerow16[0:1, 0:512],
                                       start=False, stop=True), token="lg_1")
    P.add("scalar", lambda h: h.copy(logit[:, 512:1024], lgbank),
          wait=("lg_1",), token="lg_e_1")
    def ltr(h):
        last = None
        for ti in range(8):
            last = h.transpose(psO[:, ti * 8:(ti + 1) * 8],
                               logit[0:8, ti * 128:(ti + 1) * 128], id8[:])
        return last
    P.add("tensor", ltr, wait=("lg_e_0", "lg_e_1", "consts_s", "consts", "ln1_r3"), token="ltr")
    P.add("scalar", lambda h: h.copy(logT[:], psO[:, 0:64]), wait=("ltr",), token="ltr_e")
    for ti in range(8):
        P.add("vector", lambda h, ti=ti: h.tensor_reduce(
            gm1[:, ti:ti + 1], logT[:, ti * 8:(ti + 1) * 8], axis=AX.X, op=ALU.max,
            negate=True), wait=(("ltr_e",) if ti == 0 else ()), token=f"g_m1_{ti}")
    for ti in range(8):
        P.add("vector", lambda h, ti=ti: h.tensor_scalar(
            gind[:, ti * 8:(ti + 1) * 8], logT[:, ti * 8:(ti + 1) * 8], gm1[:, ti:ti + 1],
            0.0, op0=ALU.add, op1=ALU.is_ge), token=f"g_i1_{ti}")
    for ti in range(8):
        P.add("vector", lambda h, ti=ti: h.scalar_tensor_tensor(
            gtmp[:, ti * 8:(ti + 1) * 8], gind[:, ti * 8:(ti + 1) * 8], -1.0e9,
            logT[:, ti * 8:(ti + 1) * 8], op0=ALU.mult, op1=ALU.add), token=f"g_lm_{ti}")
    for ti in range(8):
        P.add("vector", lambda h, ti=ti: h.tensor_reduce(
            gden[:, ti:ti + 1], gtmp[:, ti * 8:(ti + 1) * 8], axis=AX.X, op=ALU.max,
            negate=True), token=f"g_m2_{ti}")
    for ti in range(8):
        P.add("scalar", lambda h, ti=ti: h.activation(
            ge_[:, ti * 8:(ti + 1) * 8], logT[:, ti * 8:(ti + 1) * 8], AF.Exp,
            bias=gm1[:, ti:ti + 1]),
            wait=(f"g_i1_{ti}",), token=f"g_e_{ti}")
    for ti in range(8):
        P.add("vector", lambda h, ti=ti: h.tensor_scalar(
            gind[:, ti * 8:(ti + 1) * 8], logT[:, ti * 8:(ti + 1) * 8], gden[:, ti:ti + 1],
            0.0, op0=ALU.add, op1=ALU.is_ge), token=f"g_i2_{ti}")
    for ti in range(8):
        P.add("vector", lambda h, ti=ti: h.scalar_tensor_tensor(
            gew[:, ti * 8:(ti + 1) * 8], ge_[:, ti * 8:(ti + 1) * 8], 1.0,
            gind[:, ti * 8:(ti + 1) * 8], op0=ALU.mult, op1=ALU.mult,
            accum_out=gden[:, ti:ti + 1]),
            wait=(f"g_e_{ti}",), token=f"g_ew_{ti}")
    P.add("vector", lambda h: h.drain(), inc=0)
    P.add("vector", lambda h: h.reciprocal(grec[:], gden[:]), token="g_rc")
    P.add("vector", lambda h: h.drain(), inc=0)
    for ti in range(8):
        P.add("vector", lambda h, ti=ti: h.tensor_scalar(
            gw16[:, ti * 8:(ti + 1) * 8], gew[:, ti * 8:(ti + 1) * 8], grec[:, ti:ti + 1],
            None, op0=ALU.mult), token=f"g_w_{ti}")
    def wtr(h):
        last = None
        for ti in range(8):
            last = h.transpose(psT0[0:8, ti * 128:(ti + 1) * 128],
                               gw16[:, ti * 8:(ti + 1) * 8], id16[:])
        return last

    def emit_wfc(mo, waits):
        P.add("gpsimd", lambda h, b=mo % NWC, mo=mo: h.dma_start(
            wcol[b][:].rearrange("p (c j) -> p c j", j=128),
            w_f[mo].rearrange("c p j -> p c j")),
            wait=waits, inc=16, token=f"wfc_{mo}", sem=f"wcs{mo % NWC}")

    for mo in range(NWC):
        last_user = 15 if mo == 0 else 10 + mo
        emit_wfc(mo, (f"po_g_{last_user}_1",))

    fbank4 = [ps0[:, :], psS0[:, 0:512], psP[:, :], psO[:, 0:512]]

    def emit_mmf_w(mo, p):
        gidx = mo * 2 + p
        bank = fbank4[gidx % 4]
        def mmf_w(h, mo=mo, p=p, bank=bank):
            last = None
            for ci in range(NCH):
                last = h.matmul(bank, wcol[mo % NWC][:, ci * 128:(ci + 1) * 128],
                                yf16[:, ci * NQ + p * 512: ci * NQ + (p + 1) * 512],
                                start=(ci == 0), stop=False)
            return h.matmul(bank, fbb[0:1, mo * 128:(mo + 1) * 128],
                            onerow16[0:1, 0:512], start=False, stop=False)
        waits = [f"wfc_{mo}", "consts_s", "consts", "ld_bf2"]
        if gidx >= 4:
            waits.append(f"s2_g_{gidx - 4}")
        elif gidx < 2:
            waits.append(f"xaf_e_{p}")
        elif gidx == 3:
            waits.append("ltr_e")
        P.add("tensor", mmf_w, wait=tuple(waits), inc=1)

    def emit_mmf_l(mo, p):
        gidx = mo * 2 + p
        bank = fbank4[gidx % 4]
        def mmf_l(h, mo=mo, p=p, bank=bank):
            return h.matmul(bank, bfb2[:, mo * 128:(mo + 1) * 128],
                            xaw[:, p * 512:(p + 1) * 512], start=False, stop=True)
        P.add("tensor", mmf_l, wait=(f"xaw_{p}",), token=f"pf_g_{mo}_{p}")
        P.add("vector", lambda h, mo=mo, p=p, bank=bank: h.scalar_tensor_tensor(
            sres[:, mo * NQ + p * 512: mo * NQ + (p + 1) * 512],
            yf16[:, mo * NQ + p * 512: mo * NQ + (p + 1) * 512], 1.0, bank,
            op0=ALU.mult, op1=ALU.add),
            wait=(f"pf_g_{mo}_{p}",), token=f"s2_g_{mo * 2 + p}")

    emit_mmf_w(0, 0)
    emit_mmf_w(0, 1)
    emit_mmf_w(1, 0)
    emit_mmf_w(1, 1)
    P.add("tensor", wtr, wait=("g_w_7", f"ptevac_{H - 1}_7"), token="wtr")
    P.add("scalar", lambda h: h.copy(wT_sb[:], psT0[0:8, :]), wait=("wtr",), token="wtr_e")
    for p in range(2):
        bank2 = [psS1[:, 0:512], psS1[:, 512:1024]][p]
        P.add("tensor", lambda h, bank2=bank2, p=p: h.matmul(
            bank2, etb[:], wT_sb[:, p * 512:(p + 1) * 512], start=True, stop=True),
            wait=("wtr_e", "consts_s", "consts"), token=f"wbc_{p}")
        P.add("vector", lambda h, p=p, bank2=bank2: h.tensor_tensor(
            xaw[:, p * 512:(p + 1) * 512], xaf[:, p * 512:(p + 1) * 512], bank2,
            op=ALU.mult),
            wait=(f"wbc_{p}", f"xaf_e_{p}"), token=f"xaw_{p}")
    for mo in range(2):
        emit_mmf_l(mo, 0)
        emit_mmf_l(mo, 1)
        P.alias(f"s2_{mo}", f"s2_g_{mo * 2 + 1}")
        P.alias(f"ln2_src_{mo}", f"s2_{mo}")
    emit_wfc(NWC, ("pf_g_0_1",))
    emit_wfc(NWC + 1, ("pf_g_1_1",))
    for mo in range(2, NCH):
        for p in range(2):
            emit_mmf_w(mo, p)
            emit_mmf_l(mo, p)
        P.alias(f"s2_{mo}", f"s2_g_{mo * 2 + 1}")
        P.alias(f"ln2_src_{mo}", f"s2_{mo}")
        if mo + NWC < NCH:
            emit_wfc(mo + NWC, (f"pf_g_{mo}_1",))

    for ci in range(NCH):
        emit_ln_stat("ln2", sres, ci,
                     extra=(("xaw_1", "ltr_e", "ln1_r3", "s2_g_31") if ci == 0 else ()))
    emit_ln_finish("ln2", sres, None, ystage, True)

    with ExitStack() as es:
        extra = (["gp_c", "const", "const_s", "bload", "af", "xb0", "xb1", "xb2",
                  "ob0", "ob1"] +
                 [f"wcs{i}" for i in range(NWC)])
        sems = {e: es.enter_context(nc.semaphore(f"sem_{e}")) for e in ENGINES}
        for e in extra:
            sems[e] = es.enter_context(nc.semaphore(f"sem_{e}"))
        ov0 = P.tok[f"out_{NCH - 2}"][1]
        ov1 = P.tok[f"out_{NCH - 1}"][1]
        P.add("gpsimd", lambda h: h.wait_ge(sems["ob0"], ov0), inc=0)
        P.add("gpsimd", lambda h: h.wait_ge(sems["ob1"], ov1), inc=0)
        block = es.enter_context(nc.Block())
        P.emit(block, sems)
    st.close()
    return nc


_NC_CACHE = None


def _get_nc():
    global _NC_CACHE
    if _NC_CACHE is None:
        _NC_CACHE = build_nc()
    return _NC_CACHE


def _prep_weights(I):
    f32 = np.float32

    def tiles(wT):
        return np.ascontiguousarray(wT.reshape(16, 128, 16, 128).transpose(2, 0, 1, 3))

    out = {}
    out["w_q"] = tiles(np.ascontiguousarray(np.asarray(I["L0_attn_wq"], f32).T))
    out["w_k"] = tiles(np.ascontiguousarray(np.asarray(I["L0_attn_wk"], f32).T))
    out["w_v"] = tiles(np.ascontiguousarray(np.asarray(I["L0_attn_wv"], f32).T))
    out["w_o"] = tiles(np.ascontiguousarray(np.asarray(I["L0_attn_wo"], f32).T))
    out["w_f"] = tiles(np.ascontiguousarray(np.asarray(I["L0_ffn_W"], f32).T))
    aq = np.asarray(I["L0_attn_qA"], f32).T.reshape(16, 128, 16)
    ak = np.asarray(I["L0_attn_kA"], f32).T.reshape(16, 128, 16)
    av = np.asarray(I["L0_attn_vA"], f32).T.reshape(16, 128, 16)
    z16 = np.zeros_like(aq)
    out["a_qkv"] = np.ascontiguousarray(
        np.concatenate([aq, z16, ak, z16, av], axis=2).transpose(1, 0, 2).reshape(128, 1280))
    out["a_o"] = np.ascontiguousarray(np.asarray(I["L0_attn_oA"], f32).T.reshape(16, 128, 16))
    zb = np.zeros((16, 2048), f32)
    out["b_qkv"] = np.ascontiguousarray(np.concatenate(
        [np.asarray(I["L0_attn_qB"], f32).T, zb, np.asarray(I["L0_attn_kB"], f32).T,
         zb, np.asarray(I["L0_attn_vB"], f32).T], axis=0))
    out["b_o"] = np.ascontiguousarray(np.asarray(I["L0_attn_oB"], f32).T)
    out["a_f"] = np.ascontiguousarray(
        np.asarray(I["L0_ffn_lora_A"], f32).reshape(128, 2048).T.reshape(16, 128, 128))
    out["b_f"] = np.ascontiguousarray(
        np.asarray(I["L0_ffn_lora_B"], f32).transpose(0, 2, 1).reshape(128, 2048))
    out["g_w"] = np.ascontiguousarray(np.asarray(I["L0_ffn_gate_w"], f32).T.reshape(16, 128, 8))
    out["g_b"] = np.ascontiguousarray(np.asarray(I["L0_ffn_gate_b"], f32).reshape(1, 8))
    out["f_b"] = np.ascontiguousarray(np.asarray(I["L0_ffn_b"], f32).reshape(1, 2048))
    out["nw"] = np.ascontiguousarray(np.asarray(I["L0_attn_norm_w"], f32).reshape(16, 128).T)
    out["ident"] = np.eye(128, dtype=f32)
    out["et"] = np.repeat(np.eye(8, dtype=np.float16), 16, axis=1)
    SW = np.zeros((128, 128), dtype=np.float16)
    SW[np.arange(64), np.arange(64) + 64] = -1.0
    SW[np.arange(64, 128), np.arange(64, 128) - 64] = 1.0
    out["sw"] = np.ascontiguousarray(SW.T)
    return out


_TABLES = None


def _rope_tables_host():
    global _TABLES
    if _TABLES is None:
        half = 64
        try:
            import jax
            import jax.numpy as jnp
            with jax.default_device(jax.devices("cpu")[0]):
                theta = 1.0 / 10000.0 ** (jnp.arange(half, dtype=jnp.float32) / half)
                pos = jnp.arange(-512, 4096, dtype=jnp.float32)
                freqs = pos[:, None] * theta[None, :]
                _TABLES = (np.asarray(jnp.cos(freqs)), np.asarray(jnp.sin(freqs)))
        except Exception:
            theta = 1.0 / np.float32(10000.0) ** (np.arange(half, dtype=np.float32)
                                                  / np.float32(half))
            pos = np.arange(-512, 4096, dtype=np.float32)
            freqs = (pos[:, None] * theta[None, :]).astype(np.float32)
            _TABLES = (np.cos(freqs), np.sin(freqs))
    return _TABLES


def _mask_for_core(chunk):
    m = np.zeros((128, NMT * WIN), dtype=np.float16)
    r = np.arange(128)[:, None]
    c = np.arange(WIN)[None, :]
    for mt in range(NMT):
        ti = mt
        q_local = 512 + 128 * ti + r
        k_local = 128 * ti + c
        valid = (k_local <= q_local) & (k_local >= q_local - 511)
        if chunk == 0 and mt < 4:
            valid = valid & (k_local >= 512)
        m[:, mt * WIN:(mt + 1) * WIN] = valid.astype(np.float16)
    return m


def kernel(**inputs):
    I = inputs
    x = np.asarray(I["L0_attn_x"], dtype=np.float32)
    B, L, _ = x.shape
    shared = _prep_weights(I)
    nc = _get_nc()
    in_maps = []
    for c in range(8):
        b, chunk = divmod(c, 4)
        s0 = chunk * NQ
        lo = s0 - HALO
        if lo < 0:
            xs = np.concatenate([np.zeros((HALO, D), dtype=np.float32),
                                 x[b, 0:s0 + NQ]], axis=0)
        else:
            xs = x[b, lo:s0 + NQ]
        cos_f, sin_f = _rope_tables_host()
        m = dict(shared)
        m["xT"] = np.ascontiguousarray(xs.T)
        m["cosin"] = np.ascontiguousarray(np.tile(cos_f[s0:s0 + NH].T, (2, 1)))
        m["sinin"] = np.ascontiguousarray(np.tile(sin_f[s0:s0 + NH].T, (2, 1)))
        m["mask01"] = _mask_for_core(chunk)
        in_maps.append(m)
    res = run_bass_kernel_spmd(nc, in_maps, core_ids=list(range(8)))
    out = np.empty((B, L, D), dtype=np.float32)
    for c in range(8):
        b, chunk = divmod(c, 4)
        s0 = chunk * NQ
        out[b, s0:s0 + NQ, :] = np.asarray(res.results[c]["o"]).T
    return out
